# version 04 (final)
# baseline (speedup 1.0000x reference)
import contextlib
import numpy as np
import concourse.bass as bass
import concourse.mybir as mybir
from concourse.bass_utils import run_bass_kernel_spmd

F32 = mybir.dt.float32
AF = mybir.ActivationFunctionType
ALU = mybir.AluOpType
AX = mybir.AxisListType

NCORE = 8
B, T, D = 2, 8192, 1024
N = B * T
NT = N // NCORE
EPS = 1e-5

ENG = {'pe': 'tensor', 'dve': 'vector', 'act': 'scalar', 'pool': 'gpsimd', 'sp': 'sync'}


class Pr:

    def __init__(self):
        self.nc = bass.Bass("TRN2", target_bir_lowering=False)
        self.ctx = contextlib.ExitStack()
        self.sems = {}
        self.counts = {k: 0 for k in ENG}
        self.last_wait = {k: {o: 0 for o in ENG} for k in ENG}
        for k in ENG:
            self.sems[k] = self.ctx.enter_context(self.nc.semaphore(name=f'sq_{k}'))
        self._n = 0
        self.in_names = []
        self.out_names = []

    def inp(self, name, shape):
        self.in_names.append(name)
        if not hasattr(self, 'in_shapes'):
            self.in_shapes = {}
        self.in_shapes[name] = list(shape)
        return self.nc.dram_tensor(name, list(shape), F32, kind="ExternalInput").ap()

    def outp(self, name, shape):
        self.out_names.append(name)
        return self.nc.dram_tensor(name, list(shape), F32, kind="ExternalOutput").ap()

    def sb(self, shape, tag="s", ctx=None):
        self._n += 1
        return (ctx or self.ctx).enter_context(
            self.nc.sbuf_tensor(f"{tag}{self._n}", list(shape), F32)).ap()

    def ps(self, shape, tag="p", ctx=None):
        self._n += 1
        return (ctx or self.ctx).enter_context(
            self.nc.psum_tensor(f"{tag}{self._n}", list(shape), F32)).ap()

    def op(self, eng, method, *args, dma=False, **kw):
        e = getattr(self.nc, ENG[eng])
        for other in ENG:
            c = self.counts[other]
            if c > self.last_wait[eng][other]:
                e.wait_ge(self.sems[other], c)
                self.last_wait[eng][other] = c
        inst = getattr(e, method)(*args, **kw)
        inc = 16 if dma else 1
        inst.then_inc(self.sems[eng], inc)
        self.counts[eng] += inc
        return inst

    def fin(self):
        e = self.nc.vector
        for other in ENG:
            c = self.counts[other]
            if c > self.last_wait['dve'][other]:
                e.wait_ge(self.sems[other], c)
                self.last_wait['dve'][other] = c
        e.engine_nop()

    def ld(self, dst, src):
        return self.op('sp', 'dma_start', dst, src, dma=True)

    def st(self, dst, src):
        return self.op('pool', 'dma_start', dst, src, dma=True)

    def tt(self, out, a, b, o=ALU.mult):
        return self.op('dve', 'tensor_tensor', out, a, b, op=o)

    def ts(self, out, a, s1, o=ALU.mult, s2=None, o2=None):
        if s2 is None:
            return self.op('dve', 'tensor_scalar', out, a, s1, None, o)
        return self.op('dve', 'tensor_scalar', out, a, s1, s2, o, o2)

    def stt(self, out, in0, scalar, in1, o0=ALU.mult, o1=ALU.add):
        return self.op('dve', 'scalar_tensor_tensor', out, in0, scalar, in1, o0, o1)

    def act(self, out, in_, f, bias=0.0, scale=1.0):
        return self.op('act', 'activation', out, in_, f, bias=bias, scale=scale)

    def red(self, out, in_, o=ALU.add, ax=AX.X):
        return self.op('dve', 'tensor_reduce', out, in_, ax, o)

    def cp(self, out, in_):
        return self.op('dve', 'tensor_copy', out, in_)

    def mm(self, out, lhsT, rhs, start=True, stop=True):
        return self.op('pe', 'matmul', out, lhsT, rhs, start=start, stop=stop)


def softplus(p, out, x, tmp):
    p.act(tmp, x, AF.Abs)
    p.act(tmp, tmp, AF.Exp, scale=-1.0)
    p.ts(tmp, tmp, 1.0, ALU.add)
    p.act(tmp, tmp, AF.Ln)
    p.ts(out, x, 0.0, ALU.max)
    p.tt(out, out, tmp, ALU.add)


def gelu_tanh(p, out, x, t1, t2):
    p.act(t1, x, AF.Square)
    p.tt(t1, t1, x, ALU.mult)
    p.stt(t1, t1, 0.044715, x, ALU.mult, ALU.add)
    p.act(t1, t1, AF.Tanh, scale=0.7978845608028654)
    p.ts(t1, t1, 1.0, ALU.add)
    p.stt(out, t1, 0.5, x, ALU.mult, ALU.mult)


class LNHelper:

    def __init__(self, p, W):
        self.p = p
        self.W = W
        self.ones = p.sb([128, 1], "lnones")
        p.op('pool', 'memset', self.ones, 1.0)
        self.ps_s = p.ps([1, W], "lns")
        self.ps_q = p.ps([1, W], "lnq")
        self.ps_a = p.ps([128, W], "lna")
        self.ps_b = p.ps([128, W], "lnb")
        self.sq = p.sb([128, W], "lnsq")
        self.mrow = p.sb([1, W], "lnm")
        self.rrow = p.sb([1, W], "lnr")
        self.ones_r = p.sb([1, 128], "lnor")
        p.op('pool', 'memset', self.ones_r, 1.0)

    def run(self, x_tiles, out_tiles, W=None, kind='ln', eps=EPS, wcols=None):
        p = self.p
        W = W or self.W
        nt = len(x_tiles)
        Dtot = nt * 128
        if kind == 'ln':
            for i, xt in enumerate(x_tiles):
                p.mm(self.ps_s[:, :W], self.ones, xt[:, :W], start=(i == 0), stop=(i == nt - 1))
        for i, xt in enumerate(x_tiles):
            p.act(self.sq[:, :W], xt[:, :W], AF.Square)
            p.mm(self.ps_q[:, :W], self.ones, self.sq[:, :W], start=(i == 0), stop=(i == nt - 1))
        if kind == 'ln':
            p.ts(self.mrow[:, :W], self.ps_s[:, :W], 1.0 / Dtot, ALU.mult)
            p.act(self.rrow[:, :W], self.mrow[:, :W], AF.Square)
            p.stt(self.rrow[:, :W], self.ps_q[:, :W], 1.0 / Dtot, self.rrow[:, :W], ALU.mult, ALU.subtract)
        else:
            p.ts(self.rrow[:, :W], self.ps_q[:, :W], 1.0 / Dtot, ALU.mult)
        p.ts(self.rrow[:, :W], self.rrow[:, :W], eps, ALU.add)
        p.act(self.rrow[:, :W], self.rrow[:, :W], AF.Ln)
        p.act(self.rrow[:, :W], self.rrow[:, :W], AF.Exp, scale=-0.5)
        p.mm(self.ps_a[:, :W], self.ones_r, self.rrow[:, :W])
        if kind == 'ln':
            p.tt(self.mrow[:, :W], self.mrow[:, :W], self.rrow[:, :W], ALU.mult)
            p.mm(self.ps_b[:, :W], self.ones_r, self.mrow[:, :W])
        for i, xt in enumerate(x_tiles):
            ot = out_tiles[i]
            p.tt(ot[:, :W], xt[:, :W], self.ps_a[:, :W], ALU.mult)
            if kind == 'ln':
                p.tt(ot[:, :W], ot[:, :W], self.ps_b[:, :W], ALU.subtract)
            if wcols is not None:
                p.ts(ot[:, :W], ot[:, :W], wcols[i], ALU.mult)


def proj(p, w_dram, x_tiles, Wtot, M, out_cb, bias_dram=None, chunk=512, wk=None, ctx=None):
    K = len(x_tiles) * 128
    nk = len(x_tiles)
    nm = (M + 127) // 128
    wt = [p.sb([128, 128], "pw", ctx=ctx) for _ in range(nk)] if wk is None else None
    bcol = p.sb([128, 1], "pb", ctx=ctx) if bias_dram is not None else None
    psum = p.ps([128, chunk], "pp", ctx=ctx)
    for mi in range(nm):
        m0 = mi * 128
        mrows = min(128, M - m0)
        if wk is None:
            for ki in range(nk):
                p.ld(wt[ki][:, :mrows], w_dram[ki * 128:(ki + 1) * 128, m0:m0 + mrows])
            wts = wt
        else:
            wts = wk[mi]
        if bias_dram is not None:
            p.ld(bcol[:mrows, :], bias_dram[m0:m0 + mrows].unsqueeze(1))
        for c0 in range(0, Wtot, chunk):
            W = min(chunk, Wtot - c0)
            for ki in range(nk):
                p.mm(psum[:mrows, :W], wts[ki][:, :mrows], x_tiles[ki][:, c0:c0 + W],
                     start=(ki == 0), stop=(ki == nk - 1))
            out_cb(mi, mrows, c0, W, psum, bcol)


def build_LA():
    p = Pr()
    W2 = 2051
    xh_d = p.inp("xh", [1024, W2])
    ipw_d = p.inp("ipw", [1024, 4256])
    dtb_d = p.inp("dtb", [32])
    cw_d = p.inp("cw", [2176, 4])
    cb_d = p.inp("cb", [2176])
    u_o = p.outp("u", [1024, 2048])
    z_o = p.outp("z", [2048, W2])
    xbc_o = p.outp("xbc", [2176, 2048])
    dt_o = p.outp("dt", [32, W2])
    xbcr_d = p.nc.dram_tensor("xbcr", [2176, W2], F32, kind="Internal").ap()

    with contextlib.ExitStack() as sub:
        xt = [p.ctx.enter_context(p.nc.sbuf_tensor(f"x{i}", [128, W2], F32)).ap() for i in range(8)]
        ut = [p.ctx.enter_context(p.nc.sbuf_tensor(f"u{i}", [128, W2], F32)).ap() for i in range(8)]
        for i in range(8):
            p.ld(xt[i], xh_d[i * 128:(i + 1) * 128, :])
        ln = LNHelper(p, 512)
        for c0 in range(0, W2, 512):
            W = min(512, W2 - c0)
            cx = [t[:, c0:c0 + W] for t in xt]
            cu = [t[:, c0:c0 + W] for t in ut]
            ln.run(cx, cu, W=W, kind='ln')
        for i in range(8):
            p.st(u_o[i * 128:(i + 1) * 128, :], ut[i][:, 3:])
        for c0 in range(0, W2, 512):
            W = min(512, W2 - c0)
            cu = [t[:, c0:c0 + W] for t in ut]
            ln.run(cu, cu, W=W, kind='rms')
        dtcol = p.sb([32, 1])
        p.ld(dtcol, dtb_d.unsqueeze(1))
        scr = p.sb([128, 512])
        scr2 = p.sb([128, 512])

        def cb(mi, mrows, c0, W, psum, bcol):
            m0 = mi * 128
            if m0 < 2048:
                p.cp(scr[:mrows, :W], psum[:mrows, :W])
                p.st(z_o[m0:m0 + mrows, c0:c0 + W], scr[:mrows, :W])
            elif m0 < 4224:
                p.cp(scr[:mrows, :W], psum[:mrows, :W])
                p.st(xbcr_d[m0 - 2048:m0 - 2048 + mrows, c0:c0 + W], scr[:mrows, :W])
            else:
                p.ts(scr[:32, :W], psum[:32, :W], dtcol, ALU.add)
                softplus(p, scr[:32, :W], scr[:32, :W], scr2[:32, :W])
                p.st(dt_o[:, c0:c0 + W], scr[:32, :W])
        proj(p, ipw_d, ut, W2, 4256, cb)
    cbuf = p.sb([128, W2])
    wc = p.sb([128, 4])
    bc = p.sb([128, 1])
    acc = p.sb([128, 2048])
    for i in range(17):
        p.ld(cbuf, xbcr_d[i * 128:(i + 1) * 128, :])
        p.ld(wc, cw_d[i * 128:(i + 1) * 128, :])
        p.ld(bc, cb_d[i * 128:(i + 1) * 128].unsqueeze(1))
        p.ts(acc, cbuf[:, 0:2048], wc[:, 0:1], ALU.mult)
        for j in range(1, 4):
            p.stt(acc, cbuf[:, j:j + 2048], wc[:, j:j + 1], acc, ALU.mult, ALU.add)
        p.ts(acc, acc, bc, ALU.add)
        p.act(acc, acc, AF.Silu)
        p.st(xbc_o[i * 128:(i + 1) * 128, :], acc)
    p.fin()
    return p


def build_LB():
    p = Pr()
    NP = 8
    dt8_d = p.inp("dt8", [NP, T])
    pA_d = p.inp("pA", [NP])
    pD_d = p.inp("pD", [NP, 64])
    xsh_d = p.inp("xsh", [512, T])
    bc_d = p.inp("bcm", [128, T])
    triu_d = p.inp("triuneg", [128, 128])
    iden_d = p.inp("iden", [128, 128])
    y_o = p.outp("y8", [NP, 64, T])

    triu = p.sb([128, 128]); p.ld(triu, triu_d)
    iden = p.sb([128, 128]); p.ld(iden, iden_d)
    ones_r = p.sb([1, 128]); p.op('pool', 'memset', ones_r, 1.0)
    zrow = p.sb([1, 128]); p.op('pool', 'memset', zrow, 0.0)
    astate = p.sb([1, NP]); p.op('pool', 'memset', astate, 0.0)
    negA = p.sb([1, NP])
    nt = p.sb([1, NP])
    p.ld(nt, pA_d.unsqueeze(0))
    p.act(negA, nt, AF.Exp)
    p.ts(negA, negA, -1.0, ALU.mult)
    Dcol = [p.sb([64, 1]) for _ in range(NP)]
    for q in range(NP):
        p.ld(Dcol[q], pD_d[q, :].unsqueeze(1))
    stT = [p.sb([64, 64]) for _ in range(NP)]
    for q in range(NP):
        p.op('pool', 'memset', stT[q], 0.0)
    xs_t = [p.sb([64, 128]) for _ in range(8)]
    bc_t = p.sb([128, 128])
    ps_T = p.ps([128, 128])
    xtok = p.sb([128, 128 * 4])
    btok = p.sb([128, 64])
    ct0 = p.sb([64, 128])
    dtr = p.sb([1, 128]); lndt = p.sb([1, 128]); lda = p.sb([1, 128])
    arow = p.sb([1, 128]); lma = p.sb([1, 128]); expa = p.sb([1, 128])
    ps_L = p.ps([128, 128])
    Lt = p.sb([128, 128])
    ps_G = p.ps([128, 128])
    St = p.sb([128, 128])
    ps_Y = p.ps([64, 128])
    ps_Yi = p.ps([64, 128])
    ps_E = p.ps([64, 128])
    ps_S = p.ps([64, 64])
    yis = p.sb([64, 128])
    ysb = p.sb([64, 128])
    wcol = p.sb([128, 1])
    wrow = p.sb([1, 128])
    xw = p.sb([128, 64])
    sttmp = p.sb([64, 64])
    eq = p.sb([1, 1])
    eqc = p.sb([64, 1])
    ps_E2 = p.ps([64, 1])
    for c in range(64):
        c0 = c * 128
        for i in range(8):
            p.ld(xs_t[i], xsh_d[i * 64:(i + 1) * 64, c0:c0 + 128])
        p.ld(bc_t, bc_d[:, c0:c0 + 128])
        p.cp(ct0, bc_t[64:128, :])
        p.op('pe', 'transpose', ps_T[:, :64], bc_t[0:64, :], iden[0:64, 0:64])
        p.cp(btok, ps_T[:, :64])
        for i in range(8):
            p.op('pe', 'transpose', ps_T[:, :64], xs_t[i], iden[0:64, 0:64])
            p.cp(xtok[:, i * 64:(i + 1) * 64], ps_T[:, :64])
        for q in range(NP):
            xtk = xtok[:, q * 64:(q + 1) * 64]
            xsrow = xs_t[q]
            p.ld(dtr, dt8_d[q, c0:c0 + 128].unsqueeze(0))
            p.act(lndt, dtr, AF.Ln)
            p.ts(lda, dtr, negA[:, q:q + 1], ALU.mult)
            p.op('dve', 'tensor_tensor_scan', arow, lda, zrow,
                 astate[:, q:q + 1], op0=ALU.add, op1=ALU.add)
            p.cp(astate[:, q:q + 1], arow[:, 127:128])
            p.tt(lma, lndt, arow, ALU.subtract)
            p.mm(ps_L, lma, ones_r, start=True, stop=False)
            p.mm(ps_L, ones_r, arow, start=False, stop=True)
            p.tt(Lt, ps_L, triu, ALU.add)
            p.act(Lt, Lt, AF.Exp)
            p.mm(ps_G, bc_t[0:64, :], ct0)
            p.tt(St, Lt, ps_G, ALU.mult)
            p.mm(ps_Y, xtk, St)
            p.mm(ps_Yi, stT[q], ct0)
            p.act(expa, arow, AF.Exp)
            p.mm(ps_E, ones_r[:, 0:64], expa)
            p.cp(yis, ps_Yi)
            p.tt(yis, yis, ps_E, ALU.mult)
            p.tt(ysb, yis, ps_Y, ALU.add)
            p.stt(ysb, xsrow, Dcol[q], ysb, ALU.mult, ALU.add)
            p.st(y_o[q, :, c0:c0 + 128], ysb)
            p.ts(wrow, lma, arow[:, 127:128], ALU.add)
            p.act(wrow, wrow, AF.Exp)
            p.op('pe', 'transpose', ps_T[:, 0:1], wrow, iden[0:1, 0:1])
            p.cp(wcol, ps_T[:, 0:1])
            p.act(eq, arow[:, 127:128], AF.Exp)
            p.mm(ps_E2, ones_r[:, 0:64], eq)
            p.cp(eqc, ps_E2)
            p.ts(xw, xtk, wcol, ALU.mult)
            p.mm(ps_S, btok, xw)
            p.ts(sttmp, stT[q], eqc, ALU.mult)
            p.tt(stT[q], sttmp, ps_S, ALU.add)
    p.fin()
    return p


def build_LC():
    p = Pr()
    y_d = p.inp("yt", [2048, 2048])
    z_d = p.inp("zt", [2048, 2048])
    u_d = p.inp("ut", [1024, 2048])
    mn_d = p.inp("mnw", [2048])
    opw_d = p.inp("opw", [2048, 1024])
    aw_d = p.inp("aw", [1024, 1152])
    r1_o = p.outp("r1", [1024, 2048])
    qkv_o = p.outp("qkv", [1152, 2048])

    r1 = [p.sb([128, 2048]) for _ in range(8)]
    gt = [p.sb([128, 512]) for _ in range(16)]
    zt = p.sb([128, 512])
    uc = p.sb([128, 512])
    mcols = [p.sb([128, 1]) for _ in range(16)]
    for i in range(16):
        p.ld(mcols[i], mn_d[i * 128:(i + 1) * 128].unsqueeze(1))
    ln = LNHelper(p, 512)
    wt = [p.sb([128, 128]) for _ in range(16)]
    psum = p.ps([128, 512])
    for c0 in range(0, 2048, 512):
        for i in range(16):
            p.ld(gt[i], y_d[i * 128:(i + 1) * 128, c0:c0 + 512])
            p.ld(zt, z_d[i * 128:(i + 1) * 128, c0:c0 + 512])
            p.act(zt, zt, AF.Silu)
            p.tt(gt[i], gt[i], zt, ALU.mult)
        ln.run(gt, gt, W=512, kind='rms', wcols=mcols)
        for mi in range(8):
            for ki in range(16):
                p.ld(wt[ki], opw_d[ki * 128:(ki + 1) * 128, mi * 128:(mi + 1) * 128])
            for ki in range(16):
                p.mm(psum, wt[ki], gt[ki], start=(ki == 0), stop=(ki == 15))
            p.ld(uc, u_d[mi * 128:(mi + 1) * 128, c0:c0 + 512])
            p.tt(r1[mi][:, c0:c0 + 512], uc, psum, ALU.add)
    for i in range(8):
        p.st(r1_o[i * 128:(i + 1) * 128, :], r1[i])
    for c0 in range(0, 2048, 512):
        ln.run([t[:, c0:c0 + 512] for t in r1], [t[:, c0:c0 + 512] for t in r1], W=512, kind='rms')
    scr = p.sb([128, 512])

    def cb(mi, mrows, c0, W, psum2, bcol):
        p.cp(scr[:mrows, :W], psum2[:mrows, :W])
        p.st(qkv_o[mi * 128:mi * 128 + mrows, c0:c0 + W], scr[:mrows, :W])
    proj(p, aw_d, r1, 2048, 1152, cb)
    p.fin()
    return p


def build_LD():
    p = Pr()
    NJ = 4
    q_d = p.inp("q4", [NJ, 64, T])
    k_d = p.inp("k4", [NJ, 64, T])
    v_d = p.inp("v4", [NJ, T, 64])
    tril_d = p.inp("trilneg", [128, 128])
    dm_d = p.inp("dmask", [4, 128, 512])
    att_o = p.outp("att", [NJ, T, 64])

    tril = p.sb([128, 128]); p.ld(tril, tril_d)
    dms = [p.sb([128, 512]) for _ in range(4)]
    for i in range(4):
        p.ld(dms[i], dm_d[i])
    ones_r = p.sb([1, 128]); p.op('pool', 'memset', ones_r, 1.0)
    ones_c = p.sb([128, 1]); p.op('pool', 'memset', ones_c, 1.0)
    iden = p.sb([128, 128])
    p.op('pool', 'memset', iden, 0.0)
    iden_d = p.inp("iden", [128, 128])
    p.ld(iden, iden_d)

    kT = p.sb([64, T])
    qT = p.sb([64, T])
    vt = [p.sb([128, 64]) for _ in range(64)]
    psM = p.ps([128, 512])
    rm = p.sb([128, 1]); tcol = p.sb([128, 1])
    mscr = p.sb([128, 128])
    nmrow = p.sb([1, 512])
    psS = p.ps([128, 512])
    Psb = p.sb([128, 512])
    psD = p.ps([1, 512])
    psAV = [p.ps([128, 64]) for _ in range(4)]
    psT = p.ps([128, 128])
    lrec = p.sb([1, 512])
    lcol = p.sb([128, 1])
    osb = p.sb([128, 64])
    for j in range(NJ):
        p.ld(kT, k_d[j])
        p.ld(qT, q_d[j])
        for i in range(64):
            p.ld(vt[i], v_d[j, i * 128:(i + 1) * 128, :])
        for c0 in range(0, T, 512):
            W = min(512, T - c0)
            p.ts(qT[:, c0:c0 + W], qT[:, c0:c0 + W], 0.125, ALU.mult)
        for tq in range(16):
            t0 = tq * 512
            for sub in range(4):
                tau = t0 + sub * 128
                qs = qT[:, tau:tau + 128]
                first = True
                for f0 in range(0, (tau // 512) * 512, 512):
                    p.mm(psM, qs, kT[:, f0:f0 + 512])
                    p.red(tcol, psM, ALU.max)
                    if first:
                        p.cp(rm, tcol); first = False
                    else:
                        p.tt(rm, rm, tcol, ALU.max)
                for f0 in range((tau // 512) * 512, tau, 128):
                    p.mm(psM[:, 0:128], qs, kT[:, f0:f0 + 128])
                    p.red(tcol, psM[:, 0:128], ALU.max)
                    if first:
                        p.cp(rm, tcol); first = False
                    else:
                        p.tt(rm, rm, tcol, ALU.max)
                p.mm(psM[:, 0:128], qs, kT[:, tau:tau + 128])
                p.tt(mscr, psM[:, 0:128], tril, ALU.add)
                p.red(tcol, mscr, ALU.max)
                if first:
                    p.cp(rm, tcol)
                else:
                    p.tt(rm, rm, tcol, ALU.max)
                p.op('pe', 'transpose', psT[0:1, :], rm, iden)
                p.ts(nmrow[:, sub * 128:(sub + 1) * 128], psT[0:1, :], -1.0, ALU.mult)
            nts = (t0 + 512) // 128
            for st in range(nts):
                s0 = st * 128
                p.mm(psS, kT[:, s0:s0 + 128], qT[:, t0:t0 + 512], start=True, stop=False)
                p.mm(psS, ones_r, nmrow, start=False, stop=True)
                if s0 >= t0:
                    p.tt(Psb, psS, dms[(s0 - t0) // 128], ALU.add)
                    p.act(Psb, Psb, AF.Exp)
                else:
                    p.act(Psb, psS, AF.Exp)
                p.mm(psD, ones_c, Psb, start=(st == 0), stop=(st == nts - 1))
                for sub in range(4):
                    p.mm(psAV[sub], Psb[:, sub * 128:(sub + 1) * 128], vt[st],
                         start=(st == 0), stop=(st == nts - 1))
            p.op('dve', 'reciprocal', lrec, psD)
            for sub in range(4):
                p.op('pe', 'transpose', psT[:, 0:1], lrec[:, sub * 128:(sub + 1) * 128], iden[0:1, 0:1])
                p.cp(lcol, psT[:, 0:1])
                p.ts(osb, psAV[sub], lcol, ALU.mult)
                p.st(att_o[j, t0 + sub * 128:t0 + (sub + 1) * 128, :], osb)
    p.fin()
    return p


def build_LE():
    p = Pr()
    W2 = 2049
    att_d = p.inp("attT", [1024, W2])
    r1_d = p.inp("r1T", [1024, W2])
    x0_d = p.inp("x0T", [1024, 2048])
    pw_d = p.inp("pw", [1024, 1024])
    mk_d = p.inp("maak", [1024])
    mr_d = p.inp("maar", [1024])
    kw_d = p.inp("kw", [1024, 4096])
    rw_d = p.inp("rw", [1024, 1024])
    vw_d = p.inp("vw", [4096, 1024])
    gw_d = p.inp("gw", [1024, 8])
    gb_d = p.inp("gb", [8])
    x1_o = p.outp("x1", [1024, 2048])
    ln1_o = p.outp("ln1", [1024, 2048])
    m1_o = p.outp("mask1", [2048, 8])
    gp_o = p.outp("gp", [2048, 1])

    MU = float(np.sqrt(0.5))
    DEN = float(np.sqrt(1.0 / (2.0 * np.pi)))
    r2 = [p.sb([128, W2]) for _ in range(8)]
    for i in range(8):
        p.ld(r2[i], r1_d[i * 128:(i + 1) * 128, :])
    with contextlib.ExitStack() as sub:
        att = [sub.enter_context(p.nc.sbuf_tensor(f"at{i}", [128, W2], F32)).ap() for i in range(8)]
        for i in range(8):
            p.ld(att[i], att_d[i * 128:(i + 1) * 128, :])

        def cb(mi, mrows, c0, W, psum, bcol):
            p.tt(r2[mi][:, c0:c0 + W], r2[mi][:, c0:c0 + W], psum[:, :W], ALU.add)
        proj(p, pw_d, att, W2, 1024, cb, ctx=sub)
    mkc = [p.sb([128, 1]) for _ in range(8)]
    mrc = [p.sb([128, 1]) for _ in range(8)]
    for i in range(8):
        p.ld(mkc[i], mk_d[i * 128:(i + 1) * 128].unsqueeze(1))
        p.ld(mrc[i], mr_d[i * 128:(i + 1) * 128].unsqueeze(1))
    ln = LNHelper(p, 512)
    rr = [p.sb([128, 513]) for _ in range(8)]
    xkc = [p.sb([128, 512]) for _ in range(8)]
    xrc = [p.sb([128, 512]) for _ in range(8)]
    xx = p.sb([128, 512])
    kvacc = [p.sb([128, 512]) for _ in range(8)]
    kact = p.sb([128, 512])
    wtl = [p.sb([128, 128]) for _ in range(8)]
    psK = p.ps([128, 512])
    psV = p.ps([128, 512])
    psG = p.ps([128, 8])
    x0c = p.sb([128, 512])
    gwt = [p.sb([128, 8]) for _ in range(8)]
    for i in range(8):
        p.ld(gwt[i], gw_d[i * 128:(i + 1) * 128, :])
    gbr = p.sb([1, 8]); p.ld(gbr, gb_d.unsqueeze(0))
    ones_r = ln.ones_r
    mx = p.sb([128, 1]); nmx = p.sb([128, 1]); ex = p.sb([128, 8])
    sm = p.sb([128, 1]); rc = p.sb([128, 1]); pr = p.sb([128, 8])
    msk = p.sb([128, 8])
    for ci, c0 in enumerate(range(0, 2048, 512)):
        ln.run([t[:, c0:c0 + 512] for t in r2], [t[:, 0:512] for t in rr], W=512, kind='rms')
        ln.run([t[:, c0 + 512:c0 + 513] for t in r2], [t[:, 512:513] for t in rr], W=1, kind='rms')
        for i in range(8):
            p.tt(xx, rr[i][:, 0:512], rr[i][:, 1:513], ALU.subtract)
            p.stt(xkc[i], xx, mkc[i], rr[i][:, 1:513], ALU.mult, ALU.add)
            p.stt(xrc[i], xx, mrc[i], rr[i][:, 1:513], ALU.mult, ALU.add)
        for mk in range(32):
            for k in range(8):
                p.ld(wtl[k], kw_d[k * 128:(k + 1) * 128, mk * 128:(mk + 1) * 128])
            for k in range(8):
                p.mm(psK, wtl[k], xkc[k], start=(k == 0), stop=(k == 7))
            p.ts(kact, psK, 1.0 / DEN, ALU.mult, -MU / DEN, ALU.add)
            p.act(kact, kact, AF.Erf)
            p.ts(kact, kact, 0.5, ALU.mult, 0.5, ALU.add)
            for m in range(8):
                p.ld(wtl[m] if False else wtl[m], vw_d[mk * 128:(mk + 1) * 128, m * 128:(m + 1) * 128])
                p.mm(psV, wtl[m], kact, start=True, stop=True)
                if mk == 0:
                    p.cp(kvacc[m], psV)
                else:
                    p.tt(kvacc[m], kvacc[m], psV, ALU.add)
        for m in range(8):
            for k in range(8):
                p.ld(wtl[k], rw_d[k * 128:(k + 1) * 128, m * 128:(m + 1) * 128])
            for k in range(8):
                p.mm(psK, wtl[k], xrc[k], start=(k == 0), stop=(k == 7))
            p.act(kact, psK, AF.Sigmoid)
            p.tt(kvacc[m], kvacc[m], kact, ALU.mult)
            p.tt(kvacc[m], kvacc[m], r2[m][:, c0 + 1:c0 + 513], ALU.add)
            p.ld(x0c, x0_d[m * 128:(m + 1) * 128, c0:c0 + 512])
            p.tt(kvacc[m], kvacc[m], x0c, ALU.add)
            p.st(x1_o[m * 128:(m + 1) * 128, c0:c0 + 512], kvacc[m])
        ln.run(kvacc, kvacc, W=512, kind='ln')
        for m in range(8):
            p.st(ln1_o[m * 128:(m + 1) * 128, c0:c0 + 512], kvacc[m])
        for sub4 in range(4):
            s0 = sub4 * 128
            for k in range(8):
                p.mm(psG, kvacc[k][:, s0:s0 + 128], gwt[k], start=(k == 0), stop=False)
            p.mm(psG, ones_r, gbr, start=False, stop=True)
            p.red(mx, psG, ALU.max)
            p.ts(nmx, mx, -1.0, ALU.mult)
            p.act(ex, psG, AF.Exp, bias=nmx)
            p.red(sm, ex, ALU.add)
            p.op('dve', 'reciprocal', rc, sm)
            p.ts(pr, ex, rc, ALU.mult)
            p.red(mx, pr, ALU.max)
            p.ts(msk, pr, mx, ALU.is_ge)
            p.st(m1_o[c0 + s0:c0 + s0 + 128, :], msk)
            p.st(gp_o[c0 + s0:c0 + s0 + 128, :], mx)
    p.fin()
    return p


def build_LF0():
    p = Pr()
    m_d = p.inp("m1f", [N, 8])
    g_d = p.inp("gpf", [N, 1])
    u_d = p.inp("cumu", [128, 128])
    w_o = p.outp("wout", [N, 1])
    U = p.sb([128, 128]); p.ld(U, u_d)
    ones_r = p.sb([1, 128]); p.op('pool', 'memset', ones_r, 1.0)
    ones_c = p.sb([128, 1]); p.op('pool', 'memset', ones_c, 1.0)
    mt = p.sb([128, 8])
    carry = p.sb([1, 8]); p.op('pool', 'memset', carry, 0.0)
    psC = p.ps([128, 8])
    psCar = p.ps([1, 8])
    lt = p.sb([128, 8])
    kr = p.sb([128, 1])
    gc = p.sb([128, 1])
    for i in range(128):
        if i > 0:
            p.cp(carry, psCar)
        p.ld(mt, m_d[i * 128:(i + 1) * 128, :])
        p.mm(psC, U, mt, start=True, stop=False)
        p.mm(psC, ones_r, carry, start=False, stop=True)
        p.mm(psCar, ones_c, mt, start=(i == 0), stop=(i == 127))
        p.ts(lt, psC, 640.5, ALU.is_lt)
        p.tt(lt, lt, mt, ALU.mult)
        p.red(kr, lt, ALU.add)
        p.ld(gc, g_d[i * 128:(i + 1) * 128, :])
        p.tt(kr, kr, gc, ALU.mult)
        p.st(w_o[i * 128:(i + 1) * 128, :], kr)
    p.fin()
    return p


def build_expert(kind, CAP, Din, Dh, Dout, nexp, boost=False):
    p = Pr()
    xg_ds = [p.inp(f"e{e}_xg", [Din, CAP]) for e in range(nexp)]
    eo_d = p.inp("eog", [Din, CAP]) if boost else None
    names = []
    for e in range(nexp):
        pre = f"e{e}_"
        if kind == 'swiglu':
            names.append((p.inp(pre + "w1", [Din, Dh]), p.inp(pre + "b1", [Dh]),
                          p.inp(pre + "w3", [Din, Dh]), p.inp(pre + "b3", [Dh]),
                          p.inp(pre + "w2", [Dh, Dout]), p.inp(pre + "b2", [Dout])))
        else:
            names.append((p.inp(pre + "w1", [Din, Dh]), p.inp(pre + "b1", [Dh]),
                          p.inp(pre + "w2", [Dh, Dout]), p.inp(pre + "b2", [Dout])))
    wr_d = p.inp("wrow", [nexp, CAP]) if kind in ('gelu', 'swiglu') else None
    y_o = p.outp("yg", [nexp, Dout, CAP])

    nkin = Din // 128
    nh = Dh // 128
    nout = Dout // 128
    CH = 512
    xt = [p.sb([128, CH]) for _ in range(nkin)]
    ht = [p.sb([128, CH]) for _ in range(nh)]
    h3 = p.sb([128, CH])
    wt = [p.sb([128, 128]) for _ in range(max(nkin, nh))]
    bcol = p.sb([128, 1])
    psum = p.ps([128, CH])
    t1 = p.sb([128, CH]); t2 = p.sb([128, CH])
    wbr = p.sb([1, CH])
    psB = p.ps([128, CH])
    ones_r = p.sb([1, 128]); p.op('pool', 'memset', ones_r, 1.0)
    et0 = p.sb([128, CH]) if boost else None
    for e in range(nexp):
        wts = names[e]
        for c0 in range(0, CAP, CH):
            W = min(CH, CAP - c0)
            for i in range(nkin):
                p.ld(xt[i][:, :W], xg_ds[e][i * 128:(i + 1) * 128, c0:c0 + W])
                if boost:
                    p.ld(et0[:, :W], eo_d[i * 128:(i + 1) * 128, c0:c0 + W])
                    p.stt(xt[i][:, :W], et0[:, :W], 0.5, xt[i][:, :W], ALU.mult, ALU.add)
            for mh in range(nh):
                for k in range(nkin):
                    p.ld(wt[k], wts[0][k * 128:(k + 1) * 128, mh * 128:(mh + 1) * 128])
                for k in range(nkin):
                    p.mm(psum[:, :W], wt[k], xt[k][:, :W], start=(k == 0), stop=(k == nkin - 1))
                p.ld(bcol, wts[1][mh * 128:(mh + 1) * 128].unsqueeze(1))
                if kind == 'gelu':
                    p.ts(t1[:, :W], psum[:, :W], bcol, ALU.add)
                    gelu_tanh(p, ht[mh][:, :W], t1[:, :W], t2[:, :W], None)
                elif kind == 'relu':
                    p.act(ht[mh][:, :W], psum[:, :W], AF.Relu, bias=bcol)
                else:
                    p.ts(t1[:, :W], psum[:, :W], bcol, ALU.add)
                    for k in range(nkin):
                        p.ld(wt[k], wts[2][k * 128:(k + 1) * 128, mh * 128:(mh + 1) * 128])
                    for k in range(nkin):
                        p.mm(psum[:, :W], wt[k], xt[k][:, :W], start=(k == 0), stop=(k == nkin - 1))
                    p.ld(bcol, wts[3][mh * 128:(mh + 1) * 128].unsqueeze(1))
                    p.ts(h3[:, :W], psum[:, :W], bcol, ALU.add)
                    p.act(t1[:, :W], t1[:, :W], AF.Silu)
                    p.tt(ht[mh][:, :W], t1[:, :W], h3[:, :W], ALU.mult)
            w2i = 4 if kind == 'swiglu' else 2
            if wr_d is not None:
                p.ld(wbr[:, :W], wr_d[e, c0:c0 + W].unsqueeze(0))
                p.mm(psB[:, :W], ones_r, wbr[:, :W])
            for md in range(nout):
                for k in range(nh):
                    p.ld(wt[k], wts[w2i][k * 128:(k + 1) * 128, md * 128:(md + 1) * 128])
                for k in range(nh):
                    p.mm(psum[:, :W], wt[k], ht[k][:, :W], start=(k == 0), stop=(k == nh - 1))
                p.ld(bcol, wts[w2i + 1][md * 128:(md + 1) * 128].unsqueeze(1))
                p.ts(t1[:, :W], psum[:, :W], bcol, ALU.add)
                if wr_d is not None:
                    p.tt(t1[:, :W], t1[:, :W], psB[:, :W], ALU.mult)
                p.st(y_o[e, md * 128:(md + 1) * 128, c0:c0 + W], t1[:, :W])
    p.fin()
    return p


def build_LG():
    p = Pr()
    x1_d = p.inp("x1T", [1024, 2048])
    ym_d = p.inp("ymoe", [1024, 2048])
    win_d = p.inp("winT", [1024, 1024])
    wkk_d = p.inp("wkkT", [1024, 128])
    wkv_d = p.inp("wkvT", [1024, 128])
    wb_d = p.inp("wbT", [1024, 128])
    bb_d = p.inp("bbeta", [128])
    wq_d = p.inp("wqT", [1024, 32])
    wr_d = p.inp("wrT", [1024, 4])
    x2_o = p.outp("x2", [1024, 2048])
    kn_o = p.outp("kn", [2048, 128])
    v_o = p.outp("vv", [2048, 128])
    be_o = p.outp("beta", [2048, 128])
    q_o = p.outp("qq", [2048, 32])
    uw_o = p.outp("uw", [2048, 4])
    sm_o = p.outp("sm", [2048, 4])

    xt = [p.sb([128, 2048]) for _ in range(8)]
    yc = p.sb([128, 2048])
    for i in range(8):
        p.ld(xt[i], x1_d[i * 128:(i + 1) * 128, :])
        p.ld(yc, ym_d[i * 128:(i + 1) * 128, :])
        p.tt(xt[i], xt[i], yc, ALU.add)
        p.st(x2_o[i * 128:(i + 1) * 128, :], xt[i])
    ln = LNHelper(p, 512)
    for c0 in range(0, 2048, 512):
        ln.run([t[:, c0:c0 + 512] for t in xt], [t[:, c0:c0 + 512] for t in xt], W=512, kind='ln')
    xf = [p.sb([128, 2048]) for _ in range(8)]

    def cb(mi, mrows, c0, W, psum, bcol):
        p.act(xf[mi][:, c0:c0 + W], psum[:, :W], AF.Silu)
    proj(p, win_d, xt, 2048, 1024, cb)
    wk = [p.sb([128, 128]) for _ in range(8)]
    wv = [p.sb([128, 128]) for _ in range(8)]
    wb = [p.sb([128, 128]) for _ in range(8)]
    wq = [p.sb([128, 32]) for _ in range(8)]
    wr = [p.sb([128, 4]) for _ in range(8)]
    for k in range(8):
        p.ld(wk[k], wkk_d[k * 128:(k + 1) * 128, :])
        p.ld(wv[k], wkv_d[k * 128:(k + 1) * 128, :])
        p.ld(wb[k], wb_d[k * 128:(k + 1) * 128, :])
        p.ld(wq[k], wq_d[k * 128:(k + 1) * 128, :])
        p.ld(wr[k], wr_d[k * 128:(k + 1) * 128, :])
    bbr = p.sb([1, 128]); p.ld(bbr, bb_d.unsqueeze(0))
    psA = p.ps([128, 128])
    Ks = p.sb([128, 128])
    sq = p.sb([128, 128])
    rg = p.sb([128, 4])
    rc4 = p.sb([128, 4])
    mx = p.sb([128, 1]); nmx = p.sb([128, 1]); ex = p.sb([128, 4])
    sm1 = p.sb([128, 1]); rc1 = p.sb([128, 1]); pr = p.sb([128, 4])
    msk = p.sb([128, 4]); l2 = p.sb([128, 4]); uwn = p.sb([128, 4])
    ones_r = ln.ones_r
    for s in range(16):
        s0 = s * 128
        for k in range(8):
            p.mm(psA, xf[k][:, s0:s0 + 128], wk[k], start=(k == 0), stop=(k == 7))
        p.cp(Ks, psA)
        p.act(sq, Ks, AF.Square)
        p.red(rg.unsqueeze(2), sq.rearrange("p (g j) -> p g j", g=4), ALU.add)
        p.act(rg, rg, AF.Sqrt)
        p.ts(rg, rg, 1e-6, ALU.add)
        p.op('dve', 'reciprocal', rc4, rg)
        p.tt(Ks.rearrange("p (g j) -> p g j", g=4), Ks.rearrange("p (g j) -> p g j", g=4),
             rc4.unsqueeze(2).broadcast_to((128, 4, 32)), ALU.mult)
        p.st(kn_o[s0:s0 + 128, :], Ks)
        for k in range(8):
            p.mm(psA, xf[k][:, s0:s0 + 128], wv[k], start=(k == 0), stop=(k == 7))
        p.cp(Ks, psA)
        p.st(v_o[s0:s0 + 128, :], Ks)
        for k in range(8):
            p.mm(psA, xf[k][:, s0:s0 + 128], wb[k], start=(k == 0), stop=False)
        p.mm(psA, ones_r, bbr, start=False, stop=True)
        p.act(Ks, psA, AF.Sigmoid)
        p.st(be_o[s0:s0 + 128, :], Ks)
        for k in range(8):
            p.mm(psA[:, 0:32], xf[k][:, s0:s0 + 128], wq[k], start=(k == 0), stop=(k == 7))
        p.cp(Ks[:, 0:32], psA[:, 0:32])
        p.st(q_o[s0:s0 + 128, :], Ks[:, 0:32])
        for k in range(8):
            p.mm(psA[:, 0:4], xf[k][:, s0:s0 + 128], wr[k], start=(k == 0), stop=(k == 7))
        p.red(mx, psA[:, 0:4], ALU.max)
        p.ts(nmx, mx, -1.0, ALU.mult)
        p.act(ex, psA[:, 0:4], AF.Exp, bias=nmx)
        p.red(sm1, ex, ALU.add)
        p.op('dve', 'reciprocal', rc1, sm1)
        p.ts(pr, ex, rc1, ALU.mult)
        p.st(sm_o[s0:s0 + 128, :], pr)
        p.cp(l2, psA[:, 0:4])
        p.red(mx, l2, ALU.max)
        p.ts(msk, l2, mx, ALU.is_ge)
        p.stt(l2, msk, -1e30, l2, ALU.mult, ALU.add)
        p.red(mx, l2, ALU.max)
        p.ts(msk, psA[:, 0:4], mx, ALU.is_ge)
        p.tt(uwn, msk, pr, ALU.mult)
        p.red(sm1, uwn, ALU.add)
        p.ts(sm1, sm1, 1e-8, ALU.add)
        p.op('dve', 'reciprocal', rc1, sm1)
        p.ts(uwn, uwn, rc1, ALU.mult)
        p.st(uw_o[s0:s0 + 128, :], uwn)
    p.fin()
    return p


def build_LH():
    p = Pr()
    nc = p.nc
    ET = mybir.EngineType
    kn_d = p.inp("knb", [T, 128])
    vt_d = p.inp("vtb", [128, T])
    bt_d = p.inp("btb", [128, T])
    q_d = p.inp("qb", [T, 32])
    uw_d = p.inp("uwtb", [4, T])
    sm_d = p.inp("smtb", [4, T])
    E_d = p.inp("Emat", [128, 32])
    out_o = p.outp("outT", [32, T])

    S = p.sb([128, 32])
    BOs = p.sb([128, 128])
    bo_dram = p.nc.dram_tensor("bo_scratch", [128, T], F32, kind="Internal").ap()
    KNx = p.sb([128, 128 * 32])
    Qx = p.sb([128, 128 * 32])
    Vs = p.sb([128, 128])
    Bs = p.sb([128, 128])
    UWs = p.sb([128, 128])
    SMs = p.sb([128, 128])
    tmp = p.sb([128, 32])
    rr = p.sb([128, 1])
    dl = p.sb([128, 1])
    outr = p.sb([128, 32])
    stp = p.sb([128, 32])
    dsx = p.sb([128, 32])
    t3 = p.sb([128, 1])
    t4 = p.sb([128, 1])
    sg = p.sb([128, 1])
    s_dma = p.ctx.enter_context(nc.semaphore(name='s_dma'))
    s_dve = p.ctx.enter_context(nc.semaphore(name='s_dve'))
    s_act = p.ctx.enter_context(nc.semaphore(name='s_act'))
    nc.gpsimd.memset(S, 0.0).then_inc(s_dve, 1)
    DVE_IT = 128 * 12
    ACT_IT = 128 * 2
    NDMA = 19 * 16
    rva = nc.vector.alloc_register()
    rvd = nc.vector.alloc_register()
    rvv = nc.vector.alloc_register()
    rsd = nc.scalar.alloc_register()
    rsy = nc.sync.alloc_register()
    nc.vector.reg_mov(rva, 0)
    nc.vector.reg_mov(rvd, 288)
    nc.vector.reg_mov(rvv, 0)
    nc.scalar.reg_mov(rsd, -2)
    nc.sync.reg_mov(rsy, 1)
    nc.vector.wait_ge(s_dve, 1)
    with nc.Fori(0, 64, engines=[ET.DVE, ET.Activation, ET.SP]) as o:
        nc.sync.wait_ge(s_dve, rsy)
        for n in range(4):
            src = kn_d[bass.ts(o, 128), n * 32:(n + 1) * 32].unsqueeze(0).broadcast_to((32, 128, 32))
            nc.sync.dma_start(KNx[n * 32:(n + 1) * 32, :].rearrange("p (t j) -> p t j", j=32), src).then_inc(s_dma, 16)
        for n in range(4):
            src = q_d[bass.ts(o, 128), :].unsqueeze(0).broadcast_to((32, 128, 32))
            nc.sync.dma_start(Qx[n * 32:(n + 1) * 32, :].rearrange("p (t j) -> p t j", j=32), src).then_inc(s_dma, 16)
        nc.sync.dma_start(Vs, vt_d[:, bass.ts(o, 128)]).then_inc(s_dma, 16)
        nc.sync.dma_start(Bs, bt_d[:, bass.ts(o, 128)]).then_inc(s_dma, 16)
        for n in range(4):
            src = uw_d[n, bass.ts(o, 128)].unsqueeze(0).broadcast_to((32, 128))
            nc.sync.dma_start(UWs[n * 32:(n + 1) * 32, :], src).then_inc(s_dma, 16)
            src = sm_d[n, bass.ts(o, 128)].unsqueeze(0).broadcast_to((32, 128))
            nc.sync.dma_start(SMs[n * 32:(n + 1) * 32, :], src).then_inc(s_dma, 16)
        nc.sync.reg_add(rsy, rsy, DVE_IT)
        nc.sync.wait_ge(s_dve, rsy)
        nc.sync.dma_start(bo_dram[:, bass.ts(o, 128)], BOs).then_inc(s_dma, 16)
        nc.vector.wait_ge(s_dma, rvd)
        nc.vector.reg_add(rvd, rvd, NDMA)
        for k in range(128):
            knk = KNx[:, k * 32:(k + 1) * 32]
            qk = Qx[:, k * 32:(k + 1) * 32]
            def dv(f, *a, **kw):
                nc.vector.reg_add(rvv, rvv, 1)
                nc.vector.wait_ge(s_dve, rvv)
                f(*a, **kw).then_inc(s_dve, 1)
            dv(nc.vector.tensor_mul, tmp, S, knk)
            dv(nc.vector.tensor_reduce, rr, tmp, AX.X, ALU.add)
            dv(nc.vector.tensor_sub, dl, Vs[:, k:k + 1], rr)
            dv(nc.vector.tensor_scalar, outr, knk, dl, None, ALU.mult)
            dv(nc.vector.scalar_tensor_tensor, stp, S, Bs[:, k:k + 1], outr, ALU.mult, ALU.add)
            nc.scalar.reg_add(rsd, rsd, 8)
            nc.scalar.wait_ge(s_dve, rsd)
            nc.scalar.activation(dsx, stp, AF.Tanh).then_inc(s_act, 1)
            nc.vector.reg_add(rva, rva, 1)
            nc.vector.wait_ge(s_act, rva)
            dv(nc.vector.tensor_sub, outr, dsx, S)
            dv(nc.vector.scalar_tensor_tensor, S, outr, UWs[:, k:k + 1], S, ALU.mult, ALU.add)
            dv(nc.vector.tensor_mul, tmp, S, qk)
            dv(nc.vector.tensor_reduce, t3, tmp, AX.X, ALU.add)
            nc.scalar.reg_add(rsd, rsd, 4)
            nc.scalar.wait_ge(s_dve, rsd)
            nc.scalar.activation(sg, t3, AF.Sigmoid).then_inc(s_act, 1)
            nc.vector.reg_add(rva, rva, 1)
            nc.vector.wait_ge(s_act, rva)
            dv(nc.vector.tensor_mul, t4, t3, t3)
            dv(nc.vector.tensor_mul, t4, t4, sg)
            dv(nc.vector.tensor_mul, BOs[:, k:k + 1], t4, SMs[:, k:k + 1])
    nc.vector.wait_ge(s_dve, 1 + 64 * DVE_IT)
    nc.vector.wait_ge(s_dma, 64 * NDMA)
    p.op('dve', 'engine_nop')
    Em = p.sb([128, 32])
    BOf = p.sb([128, T])
    psO = p.ps([32, 512])
    osb = p.sb([32, T])
    p.ld(Em, E_d)
    p.ld(BOf, bo_dram)
    for c in range(16):
        p.mm(psO, Em, BOf[:, c * 512:(c + 1) * 512])
        p.cp(osb[:, c * 512:(c + 1) * 512], psO)
    p.st(out_o, osb)
    p.fin()
    return p


def build_LI():
    p = Pr()
    x2_d = p.inp("x2T", [1024, 2048])
    ot_d = p.inp("outTs", [32, 2048])
    wo_d = p.inp("woutT", [32, 1024])
    x3_o = p.outp("x3", [1024, 2048])
    gr_o = p.outp("groups", [1024, 128])
    xt = [p.sb([128, 2048]) for _ in range(8)]
    for i in range(8):
        p.ld(xt[i], x2_d[i * 128:(i + 1) * 128, :])
    ot = p.sb([32, 2048]); p.ld(ot, ot_d)
    wo = p.sb([32, 1024]); p.ld(wo, wo_d)
    psum = p.ps([128, 512])
    for mi in range(8):
        for c0 in range(0, 2048, 512):
            p.mm(psum, wo[:, mi * 128:(mi + 1) * 128], ot[:, c0:c0 + 512])
            p.tt(xt[mi][:, c0:c0 + 512], xt[mi][:, c0:c0 + 512], psum, ALU.add)
        p.st(x3_o[mi * 128:(mi + 1) * 128, :], xt[mi])
    ln = LNHelper(p, 512)
    for c0 in range(0, 2048, 512):
        ln.run([t[:, c0:c0 + 512] for t in xt], [t[:, c0:c0 + 512] for t in xt], W=512, kind='ln')
    gt = p.sb([128, 128])
    for i in range(8):
        p.red(gt.unsqueeze(2), xt[i].rearrange("p (g u) -> p g u", u=16), ALU.add)
        p.ts(gt, gt, 1.0 / 16.0, ALU.mult)
        p.st(gr_o[i * 128:(i + 1) * 128, :], gt)
    p.fin()
    return p


def build_LJ1():
    p = Pr()
    g_d = p.inp("groupsb", [1024, 512])
    w1_d = p.inp("ws1", [1024, 256])
    b1_d = p.inp("bs1", [256])
    w2_d = p.inp("ws2", [256, 1])
    b2_d = p.inp("bs2", [1])
    iden_d = p.inp("iden", [128, 128])
    sel_o = p.outp("sel", [512, 1])
    gt = [p.sb([128, 512]) for _ in range(8)]
    for i in range(8):
        p.ld(gt[i], g_d[i * 128:(i + 1) * 128, :])
    iden = p.sb([128, 128]); p.ld(iden, iden_d)
    ones_r = p.sb([1, 128]); p.op('pool', 'memset', ones_r, 1.0)
    ht = [p.sb([128, 512]) for _ in range(2)]
    wt = [p.sb([128, 128]) for _ in range(8)]
    bcol = p.sb([128, 1])
    psum = p.ps([128, 512])
    for mh in range(2):
        for k in range(8):
            p.ld(wt[k], w1_d[k * 128:(k + 1) * 128, mh * 128:(mh + 1) * 128])
        for k in range(8):
            p.mm(psum, wt[k], gt[k], start=(k == 0), stop=(k == 7))
        p.ld(bcol, b1_d[mh * 128:(mh + 1) * 128].unsqueeze(1))
        p.act(ht[mh], psum, AF.Relu, bias=bcol)
    w2c = p.sb([128, 1]); b2s = p.sb([1, 1])
    p.ld(b2s, b2_d.unsqueeze(0))
    psR = p.ps([1, 512])
    for k in range(2):
        p.ld(w2c, w2_d[k * 128:(k + 1) * 128, :])
        p.mm(psR, w2c, ht[k], start=(k == 0), stop=(k == 1))
    srow = p.sb([1, 512])
    p.ts(srow, psR, b2s, ALU.add)
    mx1 = p.sb([1, 1]); nm1 = p.sb([1, 1]); er = p.sb([1, 512])
    p.red(mx1, srow, ALU.max)
    p.ts(nm1, mx1, -1.0, ALU.mult)
    p.act(er, srow, AF.Exp, bias=nm1)
    p.red(mx1, er, ALU.add)
    p.op('dve', 'reciprocal', nm1, mx1)
    p.ts(er, er, nm1, ALU.mult)
    psB = p.ps([128, 512])
    p.mm(psB, ones_r, er)
    Pb = p.sb([128, 512]); p.cp(Pb, psB)
    psT = p.ps([128, 128])
    pcol = p.sb([128, 1]); cmp = p.sb([128, 512]); rk = p.sb([128, 1])
    for t4 in range(4):
        p.op('pe', 'transpose', psT[:, 0:1], er[:, t4 * 128:(t4 + 1) * 128], iden[0:1, 0:1])
        p.cp(pcol, psT[:, 0:1])
        p.ts(cmp, Pb, pcol, ALU.is_gt)
        p.red(rk, cmp, ALU.add)
        p.ts(rk, rk, 63.5, ALU.is_lt)
        p.st(sel_o[t4 * 128:(t4 + 1) * 128, :], rk)
    p.fin()
    return p


def build_LJ2():
    p = Pr()
    s_d = p.inp("selT", [1024, 64])
    wq_d = p.inp("wqa", [1024, 1024])
    wk_d = p.inp("wka", [1024, 1024])
    wv_d = p.inp("wva", [1024, 1024])
    wo_d = p.inp("woa", [1024, 1024])
    iden_d = p.inp("iden", [128, 128])
    og_o = p.outp("outg", [1024, 64])
    st = [p.sb([128, 64]) for _ in range(8)]
    for i in range(8):
        p.ld(st[i], s_d[i * 128:(i + 1) * 128, :])
    iden = p.sb([128, 128]); p.ld(iden, iden_d)
    qt = [p.sb([128, 64]) for _ in range(8)]
    kt = [p.sb([128, 64]) for _ in range(8)]
    vt = [p.sb([128, 64]) for _ in range(8)]
    ct = [p.sb([128, 64]) for _ in range(8)]
    wt = [p.sb([128, 128]) for _ in range(8)]
    psum = p.ps([128, 64])
    for (wd, dst, scl) in [(wq_d, qt, 0.125), (wk_d, kt, None), (wv_d, vt, None)]:
        for mi in range(8):
            for k in range(8):
                p.ld(wt[k], wd[k * 128:(k + 1) * 128, mi * 128:(mi + 1) * 128])
            for k in range(8):
                p.mm(psum, wt[k], st[k], start=(k == 0), stop=(k == 7))
            if scl:
                p.ts(dst[mi], psum, scl, ALU.mult)
            else:
                p.cp(dst[mi], psum)
    psA = p.ps([64, 64])
    psT = p.ps([128, 128])
    mx = p.sb([64, 1]); ex = p.sb([64, 64]); sm = p.sb([64, 1])
    ATs = p.sb([64, 64]); Vr = p.sb([64, 64]); vscr = p.sb([64, 64])
    hqs = p.sb([64, 64]); hks = p.sb([64, 64])
    for h in range(16):
        ti, r0 = h // 2, (h % 2) * 64
        hq = qt[ti][r0:r0 + 64, :]
        hk = kt[ti][r0:r0 + 64, :]
        hv = vt[ti][r0:r0 + 64, :]
        p.cp(hqs, hq)
        p.cp(hks, hk)
        p.mm(psA, hqs, hks)
        p.red(mx, psA, ALU.max)
        p.ts(mx, mx, -1.0, ALU.mult)
        p.act(ex, psA, AF.Exp, bias=mx)
        p.red(sm, ex, ALU.add)
        p.op('dve', 'reciprocal', sm, sm)
        p.ts(ex, ex, sm, ALU.mult)
        p.op('pe', 'transpose', psT[0:64, 0:64], ex, iden[0:64, 0:64])
        p.cp(ATs, psT[0:64, 0:64])
        p.cp(vscr, hv)
        p.op('pe', 'transpose', psT[0:64, 0:64], vscr, iden[0:64, 0:64])
        p.cp(Vr, psT[0:64, 0:64])
        p.mm(psA, Vr, ATs)
        p.cp(ct[ti][r0:r0 + 64, :], psA)
    scr = p.sb([128, 64])
    for mi in range(8):
        for k in range(8):
            p.ld(wt[k], wo_d[k * 128:(k + 1) * 128, mi * 128:(mi + 1) * 128])
        for k in range(8):
            p.mm(psum, wt[k], ct[k], start=(k == 0), stop=(k == 7))
        p.cp(scr, psum)
        p.st(og_o[mi * 128:(mi + 1) * 128, :], scr)
    p.fin()
    return p


def build_LK():
    p = Pr()
    x3_d = p.inp("x3T", [1024, 2048])
    yf_d = p.inp("yffn", [1024, 2048])
    xpw_d = p.inp("xpwT", [1024, 64])
    dpw_d = p.inp("dpwT", [64, 1024])
    dpb_d = p.inp("dpb", [1024])
    bpw_d = p.inp("bpwT", [1024, 16])
    cpw_d = p.inp("cpwT", [1024, 16])
    x4_o = p.outp("x4", [1024, 2048])
    ln4_o = p.outp("ln4", [1024, 2048])
    dl_o = p.outp("delta", [1024, 2048])
    bm_o = p.outp("bm", [16, 2048])
    cm_o = p.outp("cm", [16, 2048])
    xt = [p.sb([128, 2048]) for _ in range(8)]
    yc = p.sb([128, 2048])
    for i in range(8):
        p.ld(xt[i], x3_d[i * 128:(i + 1) * 128, :])
        p.ld(yc, yf_d[i * 128:(i + 1) * 128, :])
        p.tt(xt[i], xt[i], yc, ALU.add)
        p.st(x4_o[i * 128:(i + 1) * 128, :], xt[i])
    ln = LNHelper(p, 512)
    for c0 in range(0, 2048, 512):
        ln.run([t[:, c0:c0 + 512] for t in xt], [t[:, c0:c0 + 512] for t in xt], W=512, kind='ln')
    for i in range(8):
        p.st(ln4_o[i * 128:(i + 1) * 128, :], xt[i])
    wt = [p.sb([128, 64]) for _ in range(8)]
    for k in range(8):
        p.ld(wt[k], xpw_d[k * 128:(k + 1) * 128, :])
    rsb = p.sb([64, 2048])
    psum = p.ps([128, 512])
    for c0 in range(0, 2048, 512):
        for k in range(8):
            p.mm(psum[0:64, :], wt[k], xt[k][:, c0:c0 + 512], start=(k == 0), stop=(k == 7))
        p.cp(rsb[:, c0:c0 + 512], psum[0:64, :])
    dw = p.sb([64, 128])
    bcol = p.sb([128, 1])
    scr = p.sb([128, 512]); scr2 = p.sb([128, 512])
    for mi in range(8):
        p.ld(dw, dpw_d[:, mi * 128:(mi + 1) * 128])
        p.ld(bcol, dpb_d[mi * 128:(mi + 1) * 128].unsqueeze(1))
        for c0 in range(0, 2048, 512):
            p.mm(psum, dw, rsb[:, c0:c0 + 512])
            p.ts(scr, psum, bcol, ALU.add)
            softplus(p, scr, scr, scr2)
            p.ts(scr, scr, 0.001, ALU.max)
            p.ts(scr, scr, 0.1, ALU.min)
            p.st(dl_o[mi * 128:(mi + 1) * 128, c0:c0 + 512], scr)
    wb = [p.sb([128, 16]) for _ in range(8)]
    wc2 = [p.sb([128, 16]) for _ in range(8)]
    for k in range(8):
        p.ld(wb[k], bpw_d[k * 128:(k + 1) * 128, :])
        p.ld(wc2[k], cpw_d[k * 128:(k + 1) * 128, :])
    for c0 in range(0, 2048, 512):
        for k in range(8):
            p.mm(psum[0:16, :], wb[k], xt[k][:, c0:c0 + 512], start=(k == 0), stop=(k == 7))
        p.cp(scr[0:16, :], psum[0:16, :])
        p.st(bm_o[:, c0:c0 + 512], scr[0:16, :])
        for k in range(8):
            p.mm(psum[0:16, :], wc2[k], xt[k][:, c0:c0 + 512], start=(k == 0), stop=(k == 7))
        p.cp(scr[0:16, :], psum[0:16, :])
        p.st(cm_o[:, c0:c0 + 512], scr[0:16, :])
    p.fin()
    return p


def build_LL():
    p = Pr()
    x4_d = p.inp("x4s", [256, T])
    ln_d = p.inp("lnxs", [256, T])
    dl_d = p.inp("dls", [256, T])
    bm_d = p.inp("bmb", [16, T])
    cm_d = p.inp("cmb", [16, T])
    al_d = p.inp("alogs", [256, 16])
    dv_d = p.inp("dvs", [256])
    x5_o = p.outp("x5s", [256, T])
    CH = 2048
    lnc = p.sb([128, CH]); dlc = p.sb([128, CH]); x4c = p.sb([128, CH])
    xdl = p.sb([128, CH]); dac = p.sb([128, CH]); dbx = p.sb([128, CH])
    hh = p.sb([128, CH]); yac = p.sb([128, CH])
    bbc = p.sb([128, CH]); cbc = p.sb([128, CH])
    eA = p.sb([128, 16])
    hst = p.sb([128, 16])
    Dcol = p.sb([128, 1])
    for dt in range(2):
        r0 = dt * 128
        p.ld(eA, al_d[r0:r0 + 128, :])
        p.act(eA, eA, AF.Exp)
        p.ld(Dcol, dv_d[r0:r0 + 128].unsqueeze(1))
        p.op('pool', 'memset', hst, 0.0)
        for c0 in range(0, T, CH):
            p.ld(lnc, ln_d[r0:r0 + 128, c0:c0 + CH])
            p.ld(dlc, dl_d[r0:r0 + 128, c0:c0 + CH])
            p.ld(x4c, x4_d[r0:r0 + 128, c0:c0 + CH])
            p.tt(xdl, lnc, dlc, ALU.mult)
            for n in range(16):
                p.ld(bbc, bm_d[n, c0:c0 + CH].unsqueeze(0).broadcast_to((128, CH)))
                p.ld(cbc, cm_d[n, c0:c0 + CH].unsqueeze(0).broadcast_to((128, CH)))
                p.ts(dac, dlc, eA[:, n:n + 1], ALU.mult)
                p.act(dac, dac, AF.Exp, scale=-1.0)
                p.tt(dbx, xdl, bbc, ALU.mult)
                p.op('dve', 'tensor_tensor_scan', hh, dac, dbx, hst[:, n:n + 1],
                     op0=ALU.mult, op1=ALU.add)
                p.cp(hst[:, n:n + 1], hh[:, CH - 1:CH])
                p.tt(hh, hh, cbc, ALU.mult)
                if n == 0:
                    p.cp(yac, hh)
                else:
                    p.tt(yac, yac, hh, ALU.add)
            p.stt(yac, lnc, Dcol, yac, ALU.mult, ALU.add)
            p.tt(yac, yac, x4c, ALU.add)
            p.st(x5_o[r0:r0 + 128, c0:c0 + CH], yac)
    p.fin()
    return p


def build_LM():
    p = Pr()
    x5_d = p.inp("x5T", [1024, 2048])
    wg_d = p.inp("wgT", [1024, 16])
    ln5_o = p.outp("ln5", [1024, 2048])
    sel_o = p.outp("selm", [2048, 16])
    wts_o = p.outp("wts", [2048, 16])
    xt = [p.sb([128, 2048]) for _ in range(8)]
    for i in range(8):
        p.ld(xt[i], x5_d[i * 128:(i + 1) * 128, :])
    ln = LNHelper(p, 512)
    for c0 in range(0, 2048, 512):
        ln.run([t[:, c0:c0 + 512] for t in xt], [t[:, c0:c0 + 512] for t in xt], W=512, kind='ln')
    for i in range(8):
        p.st(ln5_o[i * 128:(i + 1) * 128, :], xt[i])
    wg = [p.sb([128, 16]) for _ in range(8)]
    for k in range(8):
        p.ld(wg[k], wg_d[k * 128:(k + 1) * 128, :])
    psG = p.ps([128, 16])
    ss = p.sb([128, 16]); gs = p.sb([128, 4]); mx = p.sb([128, 1])
    m1g = p.sb([128, 4]); l2g = p.sb([128, 4]); gmask = p.sb([128, 4])
    pm = p.sb([128, 4]); mskd = p.sb([128, 16]); em = p.sb([128, 16])
    selm = p.sb([128, 16]); wsel = p.sb([128, 16]); sm1 = p.sb([128, 1])
    DBG['selm'] = selm
    for s in range(16):
        s0 = s * 128
        for k in range(8):
            p.mm(psG, xt[k][:, s0:s0 + 128], wg[k], start=(k == 0), stop=(k == 7))
        p.act(ss, psG, AF.Sigmoid)
        p.red(gs.unsqueeze(2), ss.rearrange("p (g j) -> p g j", g=4), ALU.max)
        p.red(mx, gs, ALU.max)
        p.ts(m1g, gs, mx, ALU.is_ge)
        p.stt(l2g, m1g, -1e30, gs, ALU.mult, ALU.add)
        p.red(mx, l2g, ALU.max)
        p.ts(gmask, gs, mx, ALU.is_ge)
        p.ts(pm, gmask, -1.0, ALU.add, 1e30, ALU.mult)
        p.tt(mskd.rearrange("p (g j) -> p g j", g=4), ss.rearrange("p (g j) -> p g j", g=4),
             pm.unsqueeze(2).broadcast_to((128, 4, 4)), ALU.add)
        p.op('pool', 'memset', selm, 0.0)
        for _ in range(4):
            p.red(mx, mskd, ALU.max)
            p.ts(em, mskd, mx, ALU.is_ge)
            p.tt(selm, selm, em, ALU.add)
            p.stt(mskd, em, -1e30, mskd, ALU.mult, ALU.add)
        p.tt(wsel, selm, ss, ALU.mult)
        p.red(sm1, wsel, ALU.add)
        p.op('dve', 'reciprocal', sm1, sm1)
        p.ts(wsel, wsel, sm1, ALU.mult)
        p.ts(wsel, wsel, 2.5, ALU.mult)
        p.st(sel_o[s0:s0 + 128, :], selm)
        p.st(wts_o[s0:s0 + 128, :], wsel)
    p.fin()
    return p


def build_LO():
    p = Pr()
    W2 = 2051
    x5_d = p.inp("x5h", [1024, W2])
    cs_d = [p.inp(f"ctr{i}", [1024, W2]) for i in range(4)]
    ipw_d = p.inp("ipw3", [1024, 4096])
    ipb_d = p.inp("ipb3", [4096])
    cw_d = p.inp("cw3", [2048, 4])
    cb_d = p.inp("cb3", [2048])
    xpw_d = p.inp("xpw3", [2048, 2080])
    xpb_d = p.inp("xpb3", [2080])
    dpw_d = p.inp("dpw3", [16, 2048])
    dpb_d = p.inp("dpb3", [2048])
    x6_o = p.outp("x6", [1024, 2048])
    z_o = p.outp("z3", [2048, 2048])
    xs_o = p.outp("xs3", [2048, 2048])
    dt_o = p.outp("dt3", [2048, 2048])
    bp_o = p.outp("bp3", [16, 2048])
    cp_o = p.outp("cp3", [2048, 2048])
    xcr_d = p.nc.dram_tensor("xcr", [2048, W2], F32, kind="Internal").ap()

    xs_d2 = p.nc.dram_tensor("xs_scr", [2048, 2048], F32, kind="Internal").ap()
    zr_d = p.nc.dram_tensor("zr", [2048, W2], F32, kind="Internal").ap()
    scr = p.sb([128, 512])
    dtr = p.sb([16, 2048])
    ln = LNHelper(p, 512)
    with contextlib.ExitStack() as sub:
        xt = [p.sb([128, W2], ctx=sub) for _ in range(8)]
        yc = p.sb([128, W2], ctx=sub)
        for i in range(8):
            p.ld(xt[i], x5_d[i * 128:(i + 1) * 128, :])
            for ci in range(4):
                p.ld(yc, cs_d[ci][i * 128:(i + 1) * 128, :])
                p.tt(xt[i], xt[i], yc, ALU.add)
            p.st(x6_o[i * 128:(i + 1) * 128, :], xt[i][:, 3:])
        for c0 in range(0, W2, 512):
            W = min(512, W2 - c0)
            ln.run([t[:, c0:c0 + W] for t in xt], [t[:, c0:c0 + W] for t in xt], W=W, kind='ln')

        def cb2(mi, mrows, c0, W, psum, bcol):
            m0 = mi * 128
            p.ts(scr[:, :W], psum[:, :W], bcol, ALU.add)
            if m0 < 2048:
                p.st(xcr_d[m0:m0 + 128, c0:c0 + W], scr[:, :W])
            else:
                p.st(zr_d[m0 - 2048:m0 - 2048 + 128, c0:c0 + W], scr[:, :W])
        proj(p, ipw_d, xt, W2, 4096, cb2, bias_dram=ipb_d, ctx=sub)
    with contextlib.ExitStack() as sub:
        cbuf = p.sb([128, W2], ctx=sub)
        xs1 = p.sb([128, 2048], ctx=sub)
        wc = p.sb([128, 4], ctx=sub); bcc = p.sb([128, 1], ctx=sub)
        for i in range(16):
            p.ld(cbuf, xcr_d[i * 128:(i + 1) * 128, :])
            p.ld(wc, cw_d[i * 128:(i + 1) * 128, :])
            p.ld(bcc, cb_d[i * 128:(i + 1) * 128].unsqueeze(1))
            p.ts(xs1, cbuf[:, 0:2048], wc[:, 0:1], ALU.mult)
            for j in range(1, 4):
                p.stt(xs1, cbuf[:, j:j + 2048], wc[:, j:j + 1], xs1, ALU.mult, ALU.add)
            p.ts(xs1, xs1, bcc, ALU.add)
            p.act(xs1, xs1, AF.Silu)
            p.st(xs_o[i * 128:(i + 1) * 128, :], xs1)
            p.st(xs_d2[i * 128:(i + 1) * 128, :], xs1)
            p.ld(cbuf[:, 0:2048], zr_d[i * 128:(i + 1) * 128, 3:])
            p.st(z_o[i * 128:(i + 1) * 128, :], cbuf[:, 0:2048])
    with contextlib.ExitStack() as sub:
        xsc = [p.sb([128, 512], ctx=sub) for _ in range(16)]
        wt = [p.sb([128, 128], ctx=sub) for _ in range(16)]
        bcol = p.sb([128, 1], ctx=sub)
        psum = p.ps([128, 512], ctx=sub)
        for c0 in range(0, 2048, 512):
            for k in range(16):
                p.ld(xsc[k], xs_d2[k * 128:(k + 1) * 128, c0:c0 + 512])
            for mi in range(17):
                m0 = mi * 128
                mrows = min(128, 2080 - m0)
                for k in range(16):
                    p.ld(wt[k][:, :mrows], xpw_d[k * 128:(k + 1) * 128, m0:m0 + mrows])
                for k in range(16):
                    p.mm(psum[:mrows, :], wt[k][:, :mrows], xsc[k], start=(k == 0), stop=(k == 15))
                p.ld(bcol[:mrows], xpb_d[m0:m0 + mrows].unsqueeze(1))
                p.ts(scr[:mrows, :], psum[:mrows, :], bcol[:mrows], ALU.add)
                if m0 == 0:
                    p.cp(dtr[:, c0:c0 + 512], scr[0:16, :])
                    p.st(bp_o[:, c0:c0 + 512], scr[16:32, :])
                    p.st(cp_o[0:96, c0:c0 + 512], scr[32:128, :])
                else:
                    p.st(cp_o[m0 - 32:m0 - 32 + mrows, c0:c0 + 512], scr[:mrows, :])
    dw = p.sb([16, 128]); bcol2 = p.sb([128, 1])
    scr2 = p.sb([128, 512])
    psum2 = p.ps([128, 512])
    for mi in range(16):
        p.ld(dw, dpw_d[:, mi * 128:(mi + 1) * 128])
        p.ld(bcol2, dpb_d[mi * 128:(mi + 1) * 128].unsqueeze(1))
        for c0 in range(0, 2048, 512):
            p.mm(psum2, dw, dtr[:, c0:c0 + 512])
            p.ts(scr[:, 0:512], psum2, bcol2, ALU.add)
            softplus(p, scr[:, 0:512], scr[:, 0:512], scr2)
            p.st(dt_o[mi * 128:(mi + 1) * 128, c0:c0 + 512], scr[:, 0:512])
    p.fin()
    return p


def build_LP():
    p = Pr()
    xs_d = p.inp("xss", [512, T])
    dt_d = p.inp("dts", [512, T])
    z_d = p.inp("zs", [512, T])
    cpp_d = p.inp("cps", [512, T])
    bp_d = p.inp("bpb", [16, T])
    al_d = p.inp("alog3", [512, 16])
    dv_d = p.inp("dv3", [512])
    g_o = p.outp("gs", [512, T])
    CH = 2048
    xsc = p.sb([128, CH]); dtc = p.sb([128, CH]); zc = p.sb([128, CH]); cpc = p.sb([128, CH])
    xdt = p.sb([128, CH]); dac = p.sb([128, CH]); lnc = p.sb([128, CH])
    et = p.sb([128, CH]); dbx = p.sb([128, CH]); hh = p.sb([128, CH]); hs = p.sb([128, CH])
    bbc = p.sb([128, CH])
    zz = p.sb([128, CH]); p.op('pool', 'memset', zz, 0.0)
    eA = p.sb([128, 16])
    pst = p.sb([128, 16]); hcol = p.sb([128, 16])
    Dcol = p.sb([128, 1])
    for dt in range(4):
        r0 = dt * 128
        p.ld(eA, al_d[r0:r0 + 128, :])
        p.act(eA, eA, AF.Exp)
        p.ld(Dcol, dv_d[r0:r0 + 128].unsqueeze(1))
        p.op('pool', 'memset', pst, 0.0)
        p.op('pool', 'memset', hcol, 0.0)
        for c0 in range(0, T, CH):
            p.ld(xsc, xs_d[r0:r0 + 128, c0:c0 + CH])
            p.ld(dtc, dt_d[r0:r0 + 128, c0:c0 + CH])
            p.ld(zc, z_d[r0:r0 + 128, c0:c0 + CH])
            p.ld(cpc, cpp_d[r0:r0 + 128, c0:c0 + CH])
            p.tt(xdt, xsc, dtc, ALU.mult)
            for n in range(16):
                p.ld(bbc, bp_d[n, c0:c0 + CH].unsqueeze(0).broadcast_to((128, CH)))
                p.ts(dac, dtc, eA[:, n:n + 1], ALU.mult)
                p.act(dac, dac, AF.Exp, scale=-1.0)
                p.ts(lnc, dac, 1e-8, ALU.add)
                p.act(lnc, lnc, AF.Ln)
                p.act(et[:, 0:1], pst[:, n:n + 1], AF.Exp)
                p.op('dve', 'tensor_tensor_scan', dac, lnc, zz, pst[:, n:n + 1],
                     op0=ALU.add, op1=ALU.add)
                p.cp(pst[:, n:n + 1], dac[:, CH - 1:CH])
                p.act(et[:, 1:CH], dac[:, 0:CH - 1], AF.Exp)
                p.tt(dbx, xdt, bbc, ALU.mult)
                p.tt(dbx, dbx, et, ALU.mult)
                p.op('dve', 'tensor_tensor_scan', hh, dbx, zz, hcol[:, n:n + 1],
                     op0=ALU.add, op1=ALU.add)
                p.cp(hcol[:, n:n + 1], hh[:, CH - 1:CH])
                if n == 0:
                    p.cp(hs, hh)
                else:
                    p.tt(hs, hs, hh, ALU.add)
            p.tt(hs, hs, cpc, ALU.mult)
            p.stt(hs, xsc, Dcol, hs, ALU.mult, ALU.add)
            p.act(zc, zc, AF.Silu)
            p.tt(hs, hs, zc, ALU.mult)
            p.st(g_o[r0:r0 + 128, c0:c0 + CH], hs)
    p.fin()
    return p


def build_LQ():
    p = Pr()
    g_d = p.inp("gT", [2048, 2048])
    x6_d = p.inp("x6T", [1024, 2048])
    ow_d = p.inp("opw3", [2048, 1024])
    ob_d = p.inp("opb3", [1024])
    gw_d = p.inp("gw3T", [1024, 8])
    gb_d = p.inp("gb3", [8])
    x7_o = p.outp("x7", [1024, 2048])
    ln7_o = p.outp("ln7", [1024, 2048])
    m1_o = p.outp("m1", [2048, 8])
    m2_o = p.outp("m2", [2048, 8])
    v1_o = p.outp("v1", [2048, 1])
    v2_o = p.outp("v2", [2048, 1])
    gt = [p.sb([128, 2048]) for _ in range(16)]
    for i in range(16):
        p.ld(gt[i], g_d[i * 128:(i + 1) * 128, :])
    xt = [p.sb([128, 2048]) for _ in range(8)]

    def cb(mi, mrows, c0, W, psum, bcol):
        if c0 == 0:
            p.ld(xt[mi], x6_d[mi * 128:(mi + 1) * 128, :])
        p.ts(psum[:, :W], psum[:, :W], bcol, ALU.add)
        p.tt(xt[mi][:, c0:c0 + W], xt[mi][:, c0:c0 + W], psum[:, :W], ALU.add)
    proj(p, ow_d, gt, 2048, 1024, cb, bias_dram=ob_d)
    for i in range(8):
        p.st(x7_o[i * 128:(i + 1) * 128, :], xt[i])
    ln = LNHelper(p, 512)
    for c0 in range(0, 2048, 512):
        ln.run([t[:, c0:c0 + 512] for t in xt], [t[:, c0:c0 + 512] for t in xt], W=512, kind='ln')
    for i in range(8):
        p.st(ln7_o[i * 128:(i + 1) * 128, :], xt[i])
    wg = [p.sb([128, 8]) for _ in range(8)]
    for k in range(8):
        p.ld(wg[k], gw_d[k * 128:(k + 1) * 128, :])
    gbr = p.sb([1, 8]); p.ld(gbr, gb_d.unsqueeze(0))
    ones_r = ln.ones_r
    psG = p.ps([128, 8])
    mx = p.sb([128, 1]); nmx = p.sb([128, 1]); ex = p.sb([128, 8])
    sm = p.sb([128, 1]); rc = p.sb([128, 1]); pr = p.sb([128, 8])
    m1 = p.sb([128, 8]); l2 = p.sb([128, 8]); m2 = p.sb([128, 8])
    for s in range(16):
        s0 = s * 128
        for k in range(8):
            p.mm(psG, xt[k][:, s0:s0 + 128], wg[k], start=(k == 0), stop=False)
        p.mm(psG, ones_r, gbr, start=False, stop=True)
        p.red(mx, psG, ALU.max)
        p.ts(nmx, mx, -1.0, ALU.mult)
        p.act(ex, psG, AF.Exp, bias=nmx)
        p.red(sm, ex, ALU.add)
        p.op('dve', 'reciprocal', rc, sm)
        p.ts(pr, ex, rc, ALU.mult)
        p.red(mx, pr, ALU.max)
        p.ts(m1, pr, mx, ALU.is_ge)
        p.st(v1_o[s0:s0 + 128, :], mx)
        p.st(m1_o[s0:s0 + 128, :], m1)
        p.stt(l2, m1, -1e30, pr, ALU.mult, ALU.add)
        p.red(mx, l2, ALU.max)
        p.ts(m2, l2, mx, ALU.is_ge)
        p.st(v2_o[s0:s0 + 128, :], mx)
        p.st(m2_o[s0:s0 + 128, :], m2)
    p.fin()
    return p


def build_LU():
    p = Pr()
    x7_d = p.inp("x7T", [1024, 2048])
    l7_d = p.inp("ln7T", [1024, 2048])
    e1_d = p.inp("eo1T", [1024, 2048])
    e2_d = p.inp("eo2T", [1024, 2048])
    v1_d = p.inp("v1r", [1, 2048])
    v2_d = p.inp("v2r", [1, 2048])
    lg_d = p.inp("lng", [1024])
    lb_d = p.inp("lnb", [1024])
    o_o = p.outp("x8", [1024, 2048])
    l7 = [p.sb([128, 2048]) for _ in range(8)]
    for i in range(8):
        p.ld(l7[i], l7_d[i * 128:(i + 1) * 128, :])
    e1 = p.sb([128, 512]); e2c = p.sb([128, 512]); x7c = p.sb([128, 512])
    v1r = p.sb([1, 2048]); v2r = p.sb([1, 2048])
    p.ld(v1r, v1_d); p.ld(v2r, v2_d)
    gcol = [p.sb([128, 1]) for _ in range(8)]
    bcol = [p.sb([128, 1]) for _ in range(8)]
    for i in range(8):
        p.ld(gcol[i], lg_d[i * 128:(i + 1) * 128].unsqueeze(1))
        p.ld(bcol[i], lb_d[i * 128:(i + 1) * 128].unsqueeze(1))
    ln = LNHelper(p, 512)
    psB1 = p.ps([128, 512]); psB2 = p.ps([128, 512])
    for c0 in range(0, 2048, 512):
        p.mm(psB1, ln.ones_r, v1r[:, c0:c0 + 512])
        p.mm(psB2, ln.ones_r, v2r[:, c0:c0 + 512])
        for i in range(8):
            p.ld(e1, e1_d[i * 128:(i + 1) * 128, c0:c0 + 512])
            p.ld(e2c, e2_d[i * 128:(i + 1) * 128, c0:c0 + 512])
            p.tt(e1, e1, psB1, ALU.mult)
            p.tt(e2c, e2c, psB2, ALU.mult)
            p.tt(e1, e1, e2c, ALU.add)
            p.tt(l7[i][:, c0:c0 + 512], l7[i][:, c0:c0 + 512], e1, ALU.add)
        ln.run([t[:, c0:c0 + 512] for t in l7], [t[:, c0:c0 + 512] for t in l7], W=512, kind='ln')
        for i in range(8):
            p.ts(l7[i][:, c0:c0 + 512], l7[i][:, c0:c0 + 512], gcol[i], ALU.mult)
            p.ts(l7[i][:, c0:c0 + 512], l7[i][:, c0:c0 + 512], bcol[i], ALU.add)
            p.ld(x7c, x7_d[i * 128:(i + 1) * 128, c0:c0 + 512])
            p.tt(l7[i][:, c0:c0 + 512], l7[i][:, c0:c0 + 512], x7c, ALU.add)
            p.st(o_o[i * 128:(i + 1) * 128, c0:c0 + 512], l7[i][:, c0:c0 + 512])
    p.fin()
    return p


DBG = {}


def _run(p, ims):
    return run_bass_kernel_spmd(p.nc, ims, core_ids=list(range(NCORE))).results


def _halo(full, t0, w, h):
    out = np.zeros((full.shape[0], w + h), np.float32)
    out[:, h:] = full[:, t0:t0 + w]
    nb = t0 % T
    hh = min(h, nb)
    if hh > 0:
        out[:, h - hh:h] = full[:, t0 - hh:t0]
    return out


def kernel(**I):
    C = np.ascontiguousarray
    f32 = np.float32
    ar = np.arange(128)
    iden = np.eye(128, dtype=f32)
    triuneg = np.where(ar[:, None] <= ar[None, :], 0.0, -1e30).astype(f32)
    trilneg = np.where(ar[None, :] <= ar[:, None], 0.0, -1e30).astype(f32)
    cumu = (ar[:, None] <= ar[None, :]).astype(f32)
    dmask = np.stack([np.where((ar[:, None] + k * 128) <= np.arange(512)[None, :], 0.0, -1e30)
                      for k in range(4)]).astype(f32)
    Emat = np.zeros((128, 32), f32)
    for n in range(4):
        Emat[n * 32:(n + 1) * 32, :] = np.eye(32)

    x0 = np.asarray(I["L0_attn_x"], f32)
    x0T = C(x0.reshape(N, D).T)
    sls = [slice(c * NT, (c + 1) * NT) for c in range(8)]

    p = build_LA()
    ims = [{"xh": _halo(x0T, c * NT, NT, 3), "ipw": I["L0_attn_in_proj_w"],
            "dtb": I["L0_attn_dt_bias"], "cw": I["L0_attn_conv_w"],
            "cb": I["L0_attn_conv_b"]} for c in range(8)]
    r = _run(p, ims)
    uT = np.concatenate([x["u"] for x in r], 1)
    DBG['uT'] = uT
    zT = np.concatenate([x["z"][:, 3:] for x in r], 1)
    DBG['zT'] = zT
    xbcT = np.concatenate([x["xbc"] for x in r], 1)
    DBG['xbcT'] = xbcT
    dtT = np.concatenate([x["dt"][:, 3:] for x in r], 1)
    DBG['dtT'] = dtT

    p = build_LB()
    ims = []
    for c in range(8):
        b, g = c // 4, c % 4
        ims.append({"dt8": C(dtT[g * 8:(g + 1) * 8, b * T:(b + 1) * T]),
                    "pA": C(np.asarray(I["L0_attn_A_log"], f32)[g * 8:(g + 1) * 8]),
                    "pD": C(np.repeat(np.asarray(I["L0_attn_D"], f32)[g * 8:(g + 1) * 8][:, None], 64, 1)),
                    "xsh": C(xbcT[g * 512:(g + 1) * 512, b * T:(b + 1) * T]),
                    "bcm": C(xbcT[2048:2176, b * T:(b + 1) * T]),
                    "triuneg": triuneg, "iden": iden})
    r = _run(p, ims)
    yT = np.zeros((2048, N), f32)
    DBG['yT'] = yT
    for c in range(8):
        b, g = c // 4, c % 4
        for q in range(8):
            h = g * 8 + q
            yT[h * 64:(h + 1) * 64, b * T:(b + 1) * T] = r[c]["y8"][q]

    p = build_LC()
    ims = [{"yt": C(yT[:, s]), "zt": C(zT[:, s]), "ut": C(uT[:, s]),
            "mnw": I["L0_attn_mnorm_w"], "opw": I["L0_attn_out_proj_w"],
            "aw": I["L0_attn_attn_w"]} for s in sls]
    r = _run(p, ims)
    r1T = np.concatenate([x["r1"] for x in r], 1)
    DBG['r1T'] = r1T
    qkvT = np.concatenate([x["qkv"] for x in r], 1)
    DBG['qkvT'] = qkvT

    p = build_LD()
    qTa, kTa, vTa = qkvT[0:1024], qkvT[1024:1088], qkvT[1088:1152]
    ims = []
    for c in range(8):
        q4 = np.zeros((4, 64, T), f32); k4 = np.zeros((4, 64, T), f32)
        v4 = np.zeros((4, T, 64), f32)
        for j in range(4):
            bh = c * 4 + j
            b, h = bh // 16, bh % 16
            q4[j] = qTa[h * 64:(h + 1) * 64, b * T:(b + 1) * T]
            k4[j] = kTa[:, b * T:(b + 1) * T]
            v4[j] = vTa[:, b * T:(b + 1) * T].T
        ims.append({"q4": q4, "k4": k4, "v4": v4, "trilneg": trilneg,
                    "dmask": dmask, "iden": iden})
    r = _run(p, ims)
    ymqaT = np.zeros((1024, N), f32)
    DBG['ymqaT'] = ymqaT
    for c in range(8):
        for j in range(4):
            bh = c * 4 + j
            b, h = bh // 16, bh % 16
            ymqaT[h * 64:(h + 1) * 64, b * T:(b + 1) * T] = r[c]["att"][j].T

    p = build_LE()
    ims = [{"attT": _halo(ymqaT, c * NT, NT, 1), "r1T": _halo(r1T, c * NT, NT, 1),
            "x0T": C(x0T[:, sls[c]]), "pw": I["L0_attn_proj_w"],
            "maak": I["L0_attn_time_maa_k"], "maar": I["L0_attn_time_maa_r"],
            "kw": I["L0_attn_key_w"], "rw": I["L0_attn_recept_w"],
            "vw": I["L0_attn_value_w"], "gw": I["L0_ffn_gate_w"],
            "gb": I["L0_ffn_gate_b"]} for c in range(8)]
    r = _run(p, ims)
    x1T = np.concatenate([x["x1"] for x in r], 1)
    DBG['x1T'] = x1T
    ln1T = np.concatenate([x["ln1"] for x in r], 1)
    DBG['ln1T'] = ln1T
    mask1 = np.concatenate([x["mask1"] for x in r], 0)
    DBG['mask1'] = mask1
    gp = np.concatenate([x["gp"] for x in r], 0)
    DBG['gp'] = gp

    p = build_LF0()
    ims = [{"m1f": mask1, "gpf": gp, "cumu": cumu} for _ in range(8)]
    r = _run(p, ims)
    wcap = r[0]["wout"][:, 0]
    DBG['wcap'] = wcap

    p = build_expert('gelu', 640, 1024, 4096, 1024, 1)
    idxs = []
    ims = []
    w1 = np.asarray(I["L0_ffn_w1"], f32); b1 = np.asarray(I["L0_ffn_b1"], f32)
    w2 = np.asarray(I["L0_ffn_w2"], f32); b2 = np.asarray(I["L0_ffn_b2"], f32)
    for e in range(8):
        idx = np.where((mask1[:, e] > 0.5) & (wcap > 0))[0][:640]
        idxs.append(idx)
        xg = np.zeros((1024, 640), f32)
        xg[:, :len(idx)] = ln1T[:, idx]
        wr = np.zeros((1, 640), f32)
        wr[0, :len(idx)] = wcap[idx]
        ims.append({"e0_xg": xg, "e0_w1": w1[e], "e0_b1": b1[e],
                    "e0_w2": w2[e], "e0_b2": b2[e], "wrow": wr})
    r = _run(p, ims)
    ymoeT = np.zeros((1024, N), f32)
    DBG['ymoeT'] = ymoeT
    for e in range(8):
        ymoeT[:, idxs[e]] = r[e]["yg"][0][:, :len(idxs[e])]

    p = build_LG()
    Wkv = np.asarray(I["L1_attn_W_kv"], f32)
    kidx = np.concatenate([np.arange(nb * 64, nb * 64 + 32) for nb in range(4)])
    vidx = np.concatenate([np.arange(nb * 64 + 32, nb * 64 + 64) for nb in range(4)])
    ims = [{"x1T": C(x1T[:, s]), "ymoe": C(ymoeT[:, s]),
            "winT": C(np.asarray(I["L1_attn_W_in"], f32).T),
            "wkkT": C(Wkv[kidx].T), "wkvT": C(Wkv[vidx].T),
            "wbT": C(np.asarray(I["L1_attn_W_beta"], f32).T),
            "bbeta": np.asarray(I["L1_attn_b_beta"], f32).reshape(128),
            "wqT": C(np.asarray(I["L1_attn_W_q"], f32).T),
            "wrT": C(np.asarray(I["L1_attn_W_router"], f32).T)} for s in sls]
    r = _run(p, ims)
    x2T = np.concatenate([x["x2"] for x in r], 1)
    DBG['x2T'] = x2T
    knf = np.concatenate([x["kn"] for x in r], 0)
    vvf = np.concatenate([x["vv"] for x in r], 0)
    bef = np.concatenate([x["beta"] for x in r], 0)
    qqf = np.concatenate([x["qq"] for x in r], 0)
    uwf = np.concatenate([x["uw"] for x in r], 0)
    smf = np.concatenate([x["sm"] for x in r], 0)

    p = build_LH()
    ims = []
    for c in range(8):
        b = c // 4
        bs = slice(b * T, (b + 1) * T)
        ims.append({"knb": C(knf[bs]), "vtb": C(vvf[bs].T), "btb": C(bef[bs].T),
                    "qb": C(qqf[bs]), "uwtb": C(uwf[bs].T), "smtb": C(smf[bs].T),
                    "Emat": Emat})
    r = _run(p, ims)
    outT = np.concatenate([r[0]["outT"], r[4]["outT"]], 1)
    DBG['outT'] = outT

    p = build_LI()
    ims = [{"x2T": C(x2T[:, sls[c]]), "outTs": C(outT[:, sls[c]]),
            "woutT": C(np.asarray(I["L1_attn_W_out"], f32).T)} for c in range(8)]
    r = _run(p, ims)
    x3T = np.concatenate([x["x3"] for x in r], 1)
    DBG['x3T'] = x3T
    groupsT = np.concatenate([x["groups"] for x in r], 1)
    DBG['groupsT'] = groupsT

    p = build_LJ1()
    ims = [{"groupsb": C(groupsT[:, (c // 4) * 512:(c // 4 + 1) * 512]),
            "ws1": I["L1_ffn_Ws1"], "bs1": I["L1_ffn_bs1"],
            "ws2": I["L1_ffn_Ws2"], "bs2": I["L1_ffn_bs2"], "iden": iden}
           for c in range(8)]
    r = _run(p, ims)
    selidx = [np.where(r[0]["sel"][:, 0] > 0.5)[0], np.where(r[4]["sel"][:, 0] > 0.5)[0]]
    p = build_LJ2()
    ims = [{"selT": C(groupsT[:, (c // 4) * 512 + selidx[c // 4]]),
            "wqa": I["L1_ffn_Wq"], "wka": I["L1_ffn_Wk"], "wva": I["L1_ffn_Wv"],
            "woa": I["L1_ffn_Wo"], "iden": iden} for c in range(8)]
    r = _run(p, ims)
    yffnT = np.zeros((1024, N), f32)
    DBG['yffnT'] = yffnT
    for b in range(2):
        og = r[0 if b == 0 else 4]["outg"]
        for ji, g in enumerate(selidx[b]):
            yffnT[:, b * T + g * 16:b * T + (g + 1) * 16] = og[:, ji:ji + 1]

    p = build_LK()
    ims = [{"x3T": C(x3T[:, s]), "yffn": C(yffnT[:, s]),
            "xpwT": C(np.asarray(I["L2_attn_x_proj_w"], f32).T),
            "dpwT": C(np.asarray(I["L2_attn_dt_proj_w"], f32).T),
            "dpb": I["L2_attn_dt_proj_b"],
            "bpwT": C(np.asarray(I["L2_attn_B_proj_w"], f32).T),
            "cpwT": C(np.asarray(I["L2_attn_C_proj_w"], f32).T)} for s in sls]
    r = _run(p, ims)
    x4T = np.concatenate([x["x4"] for x in r], 1)
    DBG['x4T'] = x4T
    ln4T = np.concatenate([x["ln4"] for x in r], 1)
    DBG['ln4T'] = ln4T
    dlT = np.concatenate([x["delta"] for x in r], 1)
    DBG['dlT'] = dlT
    bmT = np.concatenate([x["bm"] for x in r], 1)
    DBG['bmT'] = bmT
    cmT = np.concatenate([x["cm"] for x in r], 1)
    DBG['cmT'] = cmT

    p = build_LL()
    A2 = np.asarray(I["L2_attn_A_log"], f32); D2 = np.asarray(I["L2_attn_D"], f32)
    ims = []
    for c in range(8):
        b, ds = c // 4, (c % 4) * 256
        bs = slice(b * T, (b + 1) * T)
        ims.append({"x4s": C(x4T[ds:ds + 256, bs]), "lnxs": C(ln4T[ds:ds + 256, bs]),
                    "dls": C(dlT[ds:ds + 256, bs]), "bmb": C(bmT[:, bs]),
                    "cmb": C(cmT[:, bs]), "alogs": C(A2[ds:ds + 256]),
                    "dvs": C(D2[ds:ds + 256])})
    r = _run(p, ims)
    x5T = np.zeros((1024, N), f32)
    DBG['x5T'] = x5T
    for c in range(8):
        b, ds = c // 4, (c % 4) * 256
        x5T[ds:ds + 256, b * T:(b + 1) * T] = r[c]["x5s"]

    p = build_LM()
    ims = [{"x5T": C(x5T[:, s]), "wgT": C(np.asarray(I["L2_ffn_Wg"], f32).T)} for s in sls]
    r = _run(p, ims)
    ln5T = np.concatenate([x["ln5"] for x in r], 1)
    DBG['ln5T'] = ln5T
    selm = np.concatenate([x["selm"] for x in r], 0)
    wts = np.concatenate([x["wts"] for x in r], 0)
    DBG['wts'] = wts

    idx2 = [np.where(selm[:, e] > 0.5)[0] for e in range(16)]
    CAP2 = int(max(512, ((max(len(i) for i in idx2) + 511) // 512) * 512))
    p = build_expert('swiglu', CAP2, 1024, 1024, 1024, 2)
    W1 = np.asarray(I["L2_ffn_W1"], f32); b1a = np.asarray(I["L2_ffn_b1"], f32)
    W2a = np.asarray(I["L2_ffn_W2"], f32); b2a = np.asarray(I["L2_ffn_b2"], f32)
    W3 = np.asarray(I["L2_ffn_W3"], f32); b3a = np.asarray(I["L2_ffn_b3"], f32)
    ims = []
    for c in range(8):
        d = {}
        for ei in range(2):
            e = c * 2 + ei
            idx = idx2[e]
            xg = np.zeros((1024, CAP2), f32)
            xg[:, :len(idx)] = ln5T[:, idx]
            wr = np.zeros((CAP2,), f32)
            wr[:len(idx)] = wts[idx, e]
            d[f"e{ei}_xg"] = xg
            d[f"e{ei}_w1"] = C(W1[e].T); d[f"e{ei}_b1"] = b1a[e]
            d[f"e{ei}_w3"] = C(W3[e].T); d[f"e{ei}_b3"] = b3a[e]
            d[f"e{ei}_w2"] = C(W2a[e].T); d[f"e{ei}_b2"] = b2a[e]
            d.setdefault("wrow", np.zeros((2, CAP2), f32))
            d["wrow"][ei] = wr
        ims.append(d)
    r = _run(p, ims)
    ctr = np.zeros((4, 1024, N), f32)
    cnt = np.zeros(N, np.int64)
    for e in range(16):
        c, ei = e // 2, e % 2
        yg = r[c]["yg"][ei]
        idx = idx2[e]
        for ji, n in enumerate(idx):
            ctr[cnt[n], :, n] = yg[:, ji]
            cnt[n] += 1

    p = build_LO()
    ims = []
    for c in range(8):
        d = {"x5h": _halo(x5T, c * NT, NT, 3),
             "ipw3": I["L3_attn_in_proj_w"], "ipb3": I["L3_attn_in_proj_b"],
             "cw3": np.asarray(I["L3_attn_conv_w"], f32).reshape(2048, 4),
             "cb3": I["L3_attn_conv_b"],
             "xpw3": I["L3_attn_x_proj_w"], "xpb3": I["L3_attn_x_proj_b"],
             "dpw3": I["L3_attn_dt_proj_w"], "dpb3": I["L3_attn_dt_proj_b"]}
        for k in range(4):
            d[f"ctr{k}"] = _halo(ctr[k], c * NT, NT, 3)
        ims.append(d)
    r = _run(p, ims)
    x6T = np.concatenate([x["x6"] for x in r], 1)
    DBG['x6T'] = x6T
    z3T = np.concatenate([x["z3"] for x in r], 1)
    DBG['z3T'] = z3T
    xs3T = np.concatenate([x["xs3"] for x in r], 1)
    DBG['xs3T'] = xs3T
    dt3T = np.concatenate([x["dt3"] for x in r], 1)
    DBG['dt3T'] = dt3T
    bp3T = np.concatenate([x["bp3"] for x in r], 1)
    DBG['bp3T'] = bp3T
    cp3T = np.concatenate([x["cp3"] for x in r], 1)
    DBG['cp3T'] = cp3T

    p = build_LP()
    A3 = np.asarray(I["L3_attn_A_log"], f32); D3 = np.asarray(I["L3_attn_D"], f32)
    ims = []
    for c in range(8):
        b, ds = c // 4, (c % 4) * 512
        bs = slice(b * T, (b + 1) * T)
        ims.append({"xss": C(xs3T[ds:ds + 512, bs]), "dts": C(dt3T[ds:ds + 512, bs]),
                    "zs": C(z3T[ds:ds + 512, bs]), "cps": C(cp3T[ds:ds + 512, bs]),
                    "bpb": C(bp3T[:, bs]), "alog3": C(A3[ds:ds + 512]),
                    "dv3": C(D3[ds:ds + 512])})
    r = _run(p, ims)
    gT = np.zeros((2048, N), f32)
    DBG['gT'] = gT
    for c in range(8):
        b, ds = c // 4, (c % 4) * 512
        gT[ds:ds + 512, b * T:(b + 1) * T] = r[c]["gs"]

    p = build_LQ()
    ims = [{"gT": C(gT[:, s]), "x6T": C(x6T[:, s]),
            "opw3": I["L3_attn_out_proj_w"], "opb3": I["L3_attn_out_proj_b"],
            "gw3T": I["L3_ffn_gate_W"], "gb3": I["L3_ffn_gate_b"]} for s in sls]
    r = _run(p, ims)
    x7T = np.concatenate([x["x7"] for x in r], 1)
    DBG['x7T'] = x7T
    ln7T = np.concatenate([x["ln7"] for x in r], 1)
    DBG['ln7T'] = ln7T
    m1f = np.concatenate([x["m1"] for x in r], 0)
    DBG['m1f'] = m1f
    m2f = np.concatenate([x["m2"] for x in r], 0)
    DBG['m2f'] = m2f
    v1f = np.concatenate([x["v1"] for x in r], 0)
    DBG['v1f'] = v1f
    v2f = np.concatenate([x["v2"] for x in r], 0)
    DBG['v2f'] = v2f

    W13 = np.asarray(I["L3_ffn_W1"], f32); b13 = np.asarray(I["L3_ffn_b1"], f32)
    W23 = np.asarray(I["L3_ffn_W2"], f32); b23 = np.asarray(I["L3_ffn_b2"], f32)
    idxr1 = [np.where(m1f[:, e] > 0.5)[0] for e in range(8)]
    CAP3 = int(max(512, ((max(len(i) for i in idxr1) + 511) // 512) * 512))
    p = build_expert('relu', CAP3, 1024, 2048, 1024, 1)
    ims = []
    for e in range(8):
        xg = np.zeros((1024, CAP3), f32)
        xg[:, :len(idxr1[e])] = ln7T[:, idxr1[e]]
        ims.append({"e0_xg": xg, "e0_w1": W13[e], "e0_b1": b13[e],
                    "e0_w2": W23[e], "e0_b2": b23[e]})
    r = _run(p, ims)
    eo1T = np.zeros((1024, N), f32)
    DBG['eo1T'] = eo1T
    for e in range(8):
        eo1T[:, idxr1[e]] = r[e]["yg"][0][:, :len(idxr1[e])]

    idxr2 = [np.where(m2f[:, e] > 0.5)[0] for e in range(8)]
    CAP4 = int(max(512, ((max(len(i) for i in idxr2) + 511) // 512) * 512))
    p = build_expert('relu', CAP4, 1024, 2048, 1024, 1, boost=True)
    ims = []
    for e in range(8):
        xg = np.zeros((1024, CAP4), f32)
        xg[:, :len(idxr2[e])] = ln7T[:, idxr2[e]]
        eg = np.zeros((1024, CAP4), f32)
        eg[:, :len(idxr2[e])] = eo1T[:, idxr2[e]]
        ims.append({"e0_xg": xg, "eog": eg, "e0_w1": W13[e], "e0_b1": b13[e],
                    "e0_w2": W23[e], "e0_b2": b23[e]})
    r = _run(p, ims)
    eo2T = np.zeros((1024, N), f32)
    DBG['eo2T'] = eo2T
    for e in range(8):
        eo2T[:, idxr2[e]] = r[e]["yg"][0][:, :len(idxr2[e])]

    p = build_LU()
    ims = [{"x7T": C(x7T[:, sls[c]]), "ln7T": C(ln7T[:, sls[c]]),
            "eo1T": C(eo1T[:, sls[c]]), "eo2T": C(eo2T[:, sls[c]]),
            "v1r": C(v1f[sls[c]].T), "v2r": C(v2f[sls[c]].T),
            "lng": I["L3_ffn_ln_gamma"], "lnb": I["L3_ffn_ln_beta"]} for c in range(8)]
    r = _run(p, ims)
    x8T = np.concatenate([x["x8"] for x in r], 1)
    DBG['x8T'] = x8T
    return C(x8T.T).reshape(2, 8192, 1024).astype(np.float32)
